# version 54
# baseline (speedup 1.0000x reference)
import numpy as np
import ml_dtypes
import concourse.bass as bass
import concourse.mybir as mybir
from concourse.bass_utils import run_bass_kernel_spmd
from contextlib import ExitStack

F32 = mybir.dt.float32
BF16 = mybir.dt.bfloat16
I32 = mybir.dt.int32
I8 = mybir.dt.int8
AF = mybir.ActivationFunctionType
ALU = mybir.AluOpType
AX = mybir.AxisListType
bfnp = ml_dtypes.bfloat16

P = 128
D = 2048
TOK = 1024
NT = 8
NH = 16
FEAT3 = 6144
NI = 5504
NOT = 43
SHC = 256
SC = 0.08838834764831845
EPS = 1e-5
NCORES = 8


class SemC:
    def __init__(self, stack, nc, name):
        self.h = stack.enter_context(nc.semaphore(name))
        self.n = 0

    def inc(self, inst, k=1):
        inst.then_inc(self.h, k)
        self.n += k
        return (self, self.n)

    def incd(self, inst):
        return self.inc(inst, 16)

    def now(self):
        return (self, self.n)


def W(eng, ms):
    if not ms:
        return
    if isinstance(ms, list):
        for m in ms:
            W(eng, m)
        return
    eng.wait_ge(ms[0].h, ms[1])


def build_nc(debug=False):
    nc = bass.Bass("TRN2", target_bir_lowering=False)

    x_in = nc.dram_tensor("x_in", [TOK, D], F32, kind="ExternalInput").ap()
    wqkv_my = nc.dram_tensor("wqkv_my", [FEAT3 // NCORES, D], F32,
                             kind="ExternalInput").ap()
    wout = nc.dram_tensor("wout", [D, D], F32, kind="ExternalInput").ap()
    bout = nc.dram_tensor("bout", [D], F32, kind="ExternalInput").ap()
    lut = nc.dram_tensor("lut", [16], F32, kind="ExternalInput").ap()
    gidx = nc.dram_tensor("gidx", [SHC, NI], I8, kind="ExternalInput").ap()
    uidx = nc.dram_tensor("uidx", [SHC, NI], I8, kind="ExternalInput").ap()
    didx = nc.dram_tensor("didx", [SHC, NI], I8, kind="ExternalInput").ap()
    identc = nc.dram_tensor("identc", [P, P], BF16, kind="ExternalInput").ap()
    maskc = nc.dram_tensor("maskc", [P, P], BF16, kind="ExternalInput").ap()
    onesc = nc.dram_tensor("onesc", [1, P], BF16, kind="ExternalInput").ap()

    y_out = nc.dram_tensor("y_out", [TOK, D], F32, kind="ExternalOutput").ap()
    taps = {}
    if debug:
        for nm, shp, dt in [
            ("tap_yT", [P, NH * TOK], BF16), ("tap_attnT", [P, NH * TOK], BF16),
            ("tap_r1", [TOK, D], F32), ("tap_zT", [P, NH * TOK], BF16),
            ("tap_wg", [D, NI], BF16), ("tap_wd", [D, NI], BF16),
            ("tap_h", [P, NOT * TOK], BF16),
        ]:
            taps[nm] = nc.dram_tensor(nm, shp, dt, kind="ExternalOutput").ap()

    draw = nc.dram_tensor("draw_t", [SHC, NI], BF16)
    gTsh = nc.dram_tensor("gTsh", [SHC, NI], BF16)
    uTsh = nc.dram_tensor("uTsh", [SHC, NI], BF16)
    wgT = nc.dram_tensor("wgT", [D, NI], BF16, addr_space="Shared")
    wuT = nc.dram_tensor("wuT", [D, NI], BF16, addr_space="Shared")
    wdF = nc.dram_tensor("wdF", [D, NI], BF16, addr_space="Shared")
    wout_bf = nc.dram_tensor("wout_bf", [D, D], BF16)
    gqkv = nc.dram_tensor("gqkv", [FEAT3 // NCORES, D], BF16)
    wqkv_bf = nc.dram_tensor("wqkv_bf", [FEAT3, D], BF16, addr_space="Shared")
    r1d = nc.dram_tensor("r1d", [TOK, D], F32)

    st = ExitStack()
    e = st.enter_context
    with st:
        AR1 = e(nc.sbuf_tensor("AR1", [P, 44032], BF16))
        AR3 = e(nc.sbuf_tensor("AR3", [P, 16384], BF16))
        AR4 = e(nc.sbuf_tensor("AR4", [P, 19200], BF16))
        R1R = e(nc.sbuf_tensor("R1R", [P, 4096], F32))
        STRM = e(nc.sbuf_tensor("STRM", [P, 4096], BF16))
        SML = e(nc.sbuf_tensor("SML", [P, 8192], BF16))
        S1R = e(nc.sbuf_tensor("S1R", [P, 4096], BF16))
        STT = e(nc.sbuf_tensor("STT", [P, 512], F32))
        idc = e(nc.sbuf_tensor("idc", [P, P], BF16))
        mkc = e(nc.sbuf_tensor("mkc", [P, P], BF16))
        onc = e(nc.sbuf_tensor("onc", [1, P], BF16))
        oncol = e(nc.sbuf_tensor("oncol", [P, 1], BF16))
        oncF = e(nc.sbuf_tensor("oncF", [1, P], F32))
        lutb = e(nc.sbuf_tensor("lutb", [P, 16], F32))

        ps0 = e(nc.psum_tensor("ps0", [P, 1024], F32))
        ps1 = e(nc.psum_tensor("ps1", [P, 1024], F32))
        ps2 = e(nc.psum_tensor("ps2", [P, 1024], F32))
        ps3 = e(nc.psum_tensor("ps3", [P, 1024], F32))
        PS = [ps0, ps1, ps2, ps3]

        bout_b = AR1.ap()[0:1, 40960:43008]
        QTh = AR1.ap()[:, 0:8192]
        KTh = AR1.ap()[:, 8192:16384]
        VTh = AR1.ap()[:, 16384:24576]
        attnT = AR1.ap()[:, 24576:40960]
        h_act = AR1.ap()[:, 0:44032]
        n4buf = AR3.ap()[:, 0:4096].bitcast(F32)
        outbufs = [AR3.ap()[:, 4096 + i * 4096: 4096 + (i + 1) * 4096].bitcast(F32)
                   for i in range(2)]

        yT = AR3.ap()[:, 0:16384]
        zT = AR3.ap()[:, 0:16384]

        idx_i32 = AR4.ap()[:, 0:640].bitcast(I8)
        idxbf = AR4.ap()[:, 2560:3840]
        gtmp = AR4.ap()[:, 3840:5120]
        accsl = [AR4.ap()[:, 5120 + i * 1280: 5120 + (i + 1) * 1280] for i in range(2)]
        xrot = [AR4.ap()[:, 11008 + i * 4096: 11008 + (i + 1) * 4096].bitcast(F32)
                for i in range(2)]
        d_bf = AR4.ap()[:, 0:16384]

        r1rA = R1R.ap()[:, 0:2048]
        r1rB = R1R.ap()[:, 2048:4096]
        r1ld = [r1rA, r1rB]

        qkv_sb = [STRM.ap()[:, 0:2048], STRM.ap()[:, 2048:4096],
                  S1R.ap()[:, 0:2048], S1R.ap()[:, 2048:4096]]
        gate_sb = [STRM.ap()[:, i * 2048:(i + 1) * 2048] for i in range(2)]
        qkv_ext = [AR4.ap()[:, 11008 + i * 2048: 11008 + (i + 1) * 2048]
                   for i in range(4)]
        woutr = [AR1.ap()[:, i * 2048:(i + 1) * 2048] for i in range(8)]
        ztile = ([AR4.ap()[:, i * 2048:(i + 1) * 2048] for i in range(5)] +
                 [AR1.ap()[:, 16384 + i * 2048: 16384 + (i + 1) * 2048] for i in range(3)])
        wd_sb = [STRM.ap()[:, i * 1024:(i + 1) * 1024] for i in range(4)]
        up_sb = [S1R.ap()[:, i * 2048:(i + 1) * 2048] for i in range(2)]
        s1b = [S1R.ap()[:, i * 2048:(i + 1) * 2048] for i in range(2)]

        Wt = [SML.ap()[:, i * 512:(i + 1) * 512] for i in range(4)]
        Wm = [SML.ap()[:, 2048 + i * 512:2048 + (i + 1) * 512] for i in range(4)]
        rs = [AR4.ap()[:, 7680 + i * 512:7680 + (i + 1) * 512] for i in range(2)]
        Vs = [AR4.ap()[:, 8704 + i * 128:8704 + (i + 1) * 128] for i in range(16)]
        rbufF = SML.ap()[0:1, 4096:5120].bitcast(F32)
        rb2F = SML.ap()[0:1, 5120:7168].bitcast(F32)[:, 0:512]
        zbuf = SML.ap()[:, 0:2048]
        sgb2 = [SML.ap()[:, 2048:3072], SML.ap()[:, 4096:5120]]
        ubuf2 = [SML.ap()[:, 3072:4096], SML.ap()[:, 5120:6144]]
        ybuf = [SML.ap()[:, 4096 + i * 2048: 4096 + (i + 1) * 2048] for i in range(2)]
        stag2 = SML.ap()[:, 5120:7872]

        stt = STT.ap()
        epsc = stt[:, 0:1]

        def slot(i):
            return stt[:, 16 + i:17 + i]
        nc20 = stt[:, 4:5]
        bnst = stt[:, 224:248]
        mvt = stt[:, 248:250]
        mv2t = {tt: stt[:, 176 + 2 * tt: 178 + 2 * tt] for tt in range(NT)}
        mv3t = {tt: stt[:, 400 + 2 * tt: 402 + 2 * tt] for tt in range(NT)}
        mv4t = {tt: stt[:, 96 + 2 * tt: 98 + 2 * tt] for tt in range(NT)}

        vv = SemC(st, nc, "vv")
        ss = SemC(st, nc, "ss")
        tt_ = SemC(st, nc, "tts")
        gg = SemC(st, nc, "gg")
        s_cst = SemC(st, nc, "scst")
        s_idx = SemC(st, nc, "sidx")
        s_tsh = SemC(st, nc, "stsh")
        s_wo = [SemC(st, nc, "swo0"), SemC(st, nc, "swo1")]
        s_wo2 = [SemC(st, nc, "swo2"), SemC(st, nc, "swo3")]
        s_sp = SemC(st, nc, "ssp")
        s_sp2 = SemC(st, nc, "ssp2")
        s_zt = [SemC(st, nc, "szt0"), SemC(st, nc, "szt1"),
                SemC(st, nc, "szt2"), SemC(st, nc, "szt3")]
        s_gu = [SemC(st, nc, "sgu0"), SemC(st, nc, "sgu1")]
        s_dw = [SemC(st, nc, "sdw0"), SemC(st, nc, "sdw1"),
                SemC(st, nc, "sdw2"), SemC(st, nc, "sdw3")]
        s_out = [SemC(st, nc, "sout0"), SemC(st, nc, "sout1")]
        s_x = [SemC(st, nc, "sx0"), SemC(st, nc, "sx1")]
        s_qc = [SemC(st, nc, "sqc0"), SemC(st, nc, "sqc1")]
        s_qs = [SemC(st, nc, "sqs0"), SemC(st, nc, "sqs1")]
        s_qw = [SemC(st, nc, "sqw0"), SemC(st, nc, "sqw1"),
                SemC(st, nc, "sqw2"), SemC(st, nc, "sqw3")]
        s_q4 = None
        s_bn = [SemC(st, nc, "sbn0"), SemC(st, nc, "sbn1")]
        s_wob = SemC(st, nc, "swob")
        s_bb = SemC(st, nc, "sbb")
        s_ccq = SemC(st, nc, "sccq")
        s_ccg = SemC(st, nc, "sccg")
        s_ccu = SemC(st, nc, "sccu")
        s_ccd = SemC(st, nc, "sccd")
        s_rd = [SemC(st, nc, "srd0"), SemC(st, nc, "srd1")]
        s_st4 = SemC(st, nc, "sstg4")

        block = e(nc.Block())

        def SY(f): block.sync(f)
        def SCA(f): block.scalar(f)
        def VEC(f): block.vector(f)
        def TEN(f): block.tensor(f)
        def GPS(f): block.gpsimd(f)

        s_qA = [s_qc[0], s_qc[1], SemC(st, nc, "sqc2"), SemC(st, nc, "sqc3")]
        s_qB = [s_qs[0], s_qs[1], SemC(st, nc, "sqs2"), SemC(st, nc, "sqs3")]
        s_stg5 = s_qB + [s_st4]
        s_q8 = s_qA + s_qB
        qkv_sb8 = qkv_sb + qkv_ext
        s_wo8 = [s_wo[0], s_wo[1], s_wo2[0], s_wo2[1],
                 SemC(st, nc, "swo4"), SemC(st, nc, "swo5"),
                 SemC(st, nc, "swo6"), SemC(st, nc, "swo7")]

        M = {}

        def _(sy):
            s_cst.incd(sy.dma_start(out=idc[:, :], in_=identc))
            s_cst.incd(sy.dma_start(out=mkc[:, :], in_=maskc))
            s_cst.incd(sy.dma_start(out=onc[:1, :], in_=onesc))
            M['const'] = s_cst.incd(sy.dma_start(
                out=lutb[:, :], in_=lut[None, :].to_broadcast([P, 16])))
        SY(_)

        def _(g):
            g.memset(epsc, EPS)
            g.memset(nc20, -20.0)
            g.memset(oncol[:, :], 1.0)
            g.memset(oncF[:, :], 1.0)
            M['eps'] = gg.inc(g.engine_nop())
            M['boutbf'] = s_bb.incd(g.dma_start(out=bout_b, in_=bout[None, :]))
        GPS(_)
        M['nwob'] = 0

        for j in range(6):
            def _(g, j=j):
                if j >= 4:
                    W(g, M['cvs', j - 4])
                M['cvl', j] = s_qA[j % 4].incd(g.dma_start(
                    out=qkv_sb[j % 4][:, :], in_=wqkv_my[j * P:(j + 1) * P, :]))
            GPS(_)
            def _(sy, j=j):
                W(sy, M['cvl', j])
                M['cvs', j] = s_qw[j % 4].incd(sy.dma_start(
                    out=gqkv.ap()[j * P:(j + 1) * P, :], in_=qkv_sb[j % 4][:, :]))
                M['convlast', j % 4] = M['cvs', j]
            SY(_)
        def _(g):
            for j in range(6):
                W(g, M['cvs', j])
            M['cc', 'qkv'] = s_ccq.inc(g.collective_compute(
                "AllGather", ALU.bypass, replica_groups=[list(range(NCORES))],
                ins=[gqkv.ap().opt()], outs=[wqkv_bf.ap().opt()]))
        GPS(_)

        xsl = [xrot[0], xrot[1], r1rA, r1rB]
        stg5 = xsl + [SML.ap()[:, 4096:8192].bitcast(F32)]
        s_x4 = s_x + [s_sp, s_tsh]

        def _(s):
            for t0 in range(4):
                M['x', t0] = s_x4[t0].incd(s.dma_start(
                    out=xsl[t0][:, :], in_=x_in[t0 * P:(t0 + 1) * P, :]))
        SCA(_)

        LN1 = {tt: {k: slot(10 * tt + j) for j, k in enumerate(
            ['negm', 'lnv', 'mv0', 'mv1', 'rstd', 'negmr'])} for tt in range(NT)}

        tr_evac = []
        psbf_tr = [ps2.ap().bitcast(BF16)[:, 0:128],
                   ps3.ap().bitcast(BF16)[:, 0:128],
                   ps2.ap().bitcast(BF16)[:, 1024:1152],
                   ps3.ap().bitcast(BF16)[:, 1024:1152]]

        def emit_tr16(src_ms, srcbuf, dst, tt, mkey):
            for kc in range(16):
                def _(t, kc=kc):
                    if kc == 0:
                        W(t, src_ms)
                        W(t, M['const'])
                    if len(tr_evac) >= 4:
                        W(t, tr_evac[-4])
                    M[mkey, tt, kc] = tt_.inc(
                        t.transpose(psbf_tr[kc % 4], srcbuf[:, kc * P:(kc + 1) * P], idc[:, :]))
                TEN(_)
                def _(v, kc=kc):
                    W(v, M[mkey, tt, kc])
                    tr_evac.append(vv.inc(v.tensor_copy(
                        dst[:, kc * TOK + tt * P: kc * TOK + (tt + 1) * P], psbf_tr[kc % 4])))
                VEC(_)
            M[mkey + 'd', tt] = tr_evac[-1]

        gchunks = [(0, 5), (5, 5), (10, 5), (15, 5), (20, 5), (25, 5), (30, 5), (35, 5), (40, 3)]
        dchunks = [(0, 1280), (1280, 1280), (2560, 1280), (3840, 1280), (5120, 384)]
        gidx3 = gidx.rearrange("(a p) c -> p a c", p=P)
        uidx3 = uidx.rearrange("(a p) c -> p a c", p=P)
        didx3 = didx.rearrange("(a p) c -> p a c", p=P)
        M['idxcast_last'] = None
        M['accguard', 0] = None
        M['accguard', 1] = None

        gitems = []
        gcursor = [0]
        gmark = {}

        def _mk_idx_load(name, src, ci):
            def f(g):
                W(g, M['idxcast_last'])
                M['idx', name, ci] = s_idx.incd(g.dma_start(out=src[0], in_=src[1]))
            return ('gps', f)

        def _mk_cast(name, ci, flen, slotpar):
            def f(v):
                W(v, M['idx', name, ci])
                W(v, M['const'])
                M['idxcast_last'] = vv.inc(v.tensor_copy(idxbf[:, 0:flen], idx_i32[:, 0:flen]))
                W(v, M['accguard', slotpar])
                v.wait_ge(vv.h, vv.n)
                vv.inc(v.tensor_scalar(accsl[slotpar][:, 0:flen], idxbf[:, 0:flen], 0.0,
                                       lutb[:, 0:1], ALU.is_equal, ALU.mult))
            return ('vec', f)

        def _mk_kstep(name, ci, flen, slotpar, k):
            def f(v):
                v.wait_ge(vv.h, vv.n)
                vv.inc(v.tensor_scalar(gtmp[:, 0:flen], idxbf[:, 0:flen], float(k),
                                       lutb[:, k:k + 1], ALU.is_equal, ALU.mult))
                v.wait_ge(vv.h, vv.n)
                vv.inc(v.tensor_tensor(accsl[slotpar][:, 0:flen],
                                       accsl[slotpar][:, 0:flen], gtmp[:, 0:flen], ALU.add))
                if k == 15:
                    M['gch', name, ci] = (vv, vv.n)
            return ('vec', f)

        def _mk_bounce(name, ci, dst, flen, slotpar, srcview):
            def f(g):
                W(g, M['gch', name, ci])
                M['bnc', name, ci] = s_bn[slotpar].incd(g.dma_start(
                    out=dst, in_=srcview))
                M['accguard', slotpar] = M['bnc', name, ci]
            return ('gps', f)

        _gctr = [0]

        def _mk_matrix_items(name, src3d, dst_t):
            seq = []
            for mt in range(2):
                for q, (a0, ln) in enumerate(dchunks):
                    ci = mt * 5 + q
                    _gctr[0] += 1
                    sp = _gctr[0] % 2
                    seq.append(_mk_idx_load(name, (
                        idx_i32[:, 0:ln], src3d[:, mt, a0:a0 + ln]), ci))
                    seq.append(_mk_cast(name, ci, ln, sp))
                    for k in range(1, 16):
                        seq.append(_mk_kstep(name, ci, ln, sp, k))
                    seq.append(_mk_bounce(name, ci,
                                          dst_t.ap()[mt * P:(mt + 1) * P, a0:a0 + ln],
                                          ln, sp, accsl[sp][:, 0:ln]))
            seq.append(('mark', name))
            return seq

        gitems.extend(_mk_matrix_items('gate', gidx3, gTsh))
        gitems.extend(_mk_matrix_items('up', uidx3, uTsh))
        gitems_d = _mk_matrix_items('down', didx3, draw)
        gdcursor = [0]

        def drip(nvec):
            done = 0
            while gcursor[0] < len(gitems) and done < nvec:
                kind, fn = gitems[gcursor[0]]
                if kind == 'gps':
                    GPS(fn)
                elif kind == 'vec':
                    VEC(fn)
                    done += 1
                else:
                    M['rawdone', fn] = [s_bn[0].now(), s_bn[1].now()]
                gcursor[0] += 1

        def drip_finish(name):
            while gcursor[0] < len(gitems):
                kind, fn = gitems[gcursor[0]]
                if kind == 'mark':
                    M['rawdone', fn] = [s_bn[0].now(), s_bn[1].now()]
                    gcursor[0] += 1
                    if fn == name:
                        return
                    continue
                if kind == 'gps':
                    GPS(fn)
                else:
                    VEC(fn)
                gcursor[0] += 1

        def drip_down(nvec):
            done = 0
            while gdcursor[0] < len(gitems_d) and done < nvec:
                kind, fn = gitems_d[gdcursor[0]]
                if kind == 'gps':
                    if gdcursor[0] == 0:
                        def f0(g, fn=fn):
                            W(g, M['ztr_all'])
                            fn(g)
                        GPS(f0)
                    else:
                        GPS(fn)
                elif kind == 'vec':
                    VEC(fn)
                    done += 1
                else:
                    M['rawdone', fn] = [s_bn[0].now(), s_bn[1].now()]
                    emit_cc('down', draw, wdF, s_ccd, M['rawdone', 'down'])
                gdcursor[0] += 1

        drip(8)

        def ln1_stats(tt):
            def _(v, tt=tt):
                L = LN1[tt]
                lmv = stt[:, 16 + 10 * tt + 2: 16 + 10 * tt + 4]
                W(v, M['x', tt])
                b = None
                for c4 in range(4):
                    b = v.bn_stats(bnst[:, c4 * 6:(c4 + 1) * 6],
                                   xsl[tt % 4][:, c4 * 512:(c4 + 1) * 512])
                vv.inc(b)
                v.wait_ge(vv.h, vv.n)
                vv.inc(v.bn_aggr(lmv, bnst))
                v.wait_ge(vv.h, vv.n)
                M['ln1s', tt] = vv.inc(
                    v.tensor_scalar(L['negm'], lmv[:, 0:1], -1.0, None, ALU.mult))
            VEC(_)
            def _(s, tt=tt):
                L = LN1[tt]
                lmv = stt[:, 16 + 10 * tt + 2: 16 + 10 * tt + 4]
                W(s, M['ln1s', tt])
                W(s, M['eps'])
                M['rstd1', tt] = ss.inc(
                    s.activation(L['lnv'], lmv[:, 1:2], AF.Sqrt, bias=epsc, scale=1.0))
            SCA(_)

        for tt in range(4):
            ln1_stats(tt)
        for tt in range(NT):
            def _(v, tt=tt):
                L = LN1[tt]
                W(v, M['rstd1', tt])
                vv.inc(v.reciprocal(L['rstd'], L['lnv']))
                v.wait_ge(vv.h, vv.n)
                vv.inc(v.tensor_scalar(L['negmr'], L['negm'], L['rstd'], None, ALU.mult))
                if tt >= 2:
                    W(v, M['ytr', tt - 2, 15])
                v.wait_ge(vv.h, vv.n)
                M['ybf', tt] = vv.inc(v.tensor_scalar(
                    ybuf[tt % 2], xsl[tt % 4][:, :], L['rstd'], L['negmr'],
                    ALU.mult, ALU.add))
            VEC(_)
            def _(s, tt=tt):
                if tt + 4 < NT:
                    W(s, M['ln1s', tt])
                    W(s, M['ybf', tt])
                    M['x', tt + 4] = s_x4[tt % 4].incd(s.dma_start(
                        out=xsl[tt % 4][:, :], in_=x_in[(tt + 4) * P:(tt + 5) * P, :]))
            SCA(_)
            emit_tr16(M['ybf', tt], ybuf[tt % 2], yT, tt, 'ytr')
            if tt + 4 < NT:
                ln1_stats(tt + 4)
            drip(6)


        def emit_qkv_half(half):
            hb = half * 512
            for m in range(48):
                if half == 0:
                    def _(eng, m=m):
                        W(eng, M['cc', 'qkv'])
                        if m >= 8:
                            W(eng, M['qmm', 0, m - 8])
                        elif m < 4:
                            W(eng, M['convlast', m % 4])
                        elif m < 6:
                            W(eng, M['ybf', 4])
                        else:
                            W(eng, M['ybf', 5])
                        M['qcast', 0, m] = s_q8[m % 8].incd(eng.dma_start(
                            out=qkv_sb8[m % 8][:, :],
                            in_=wqkv_bf.ap()[m * P:(m + 1) * P, :]))
                    [GPS, SY, SCA][m % 3](_)
                else:
                    def _(eng, m=m):
                        if m >= 8:
                            W(eng, M['qmm', 1, m - 8])
                        else:
                            W(eng, M['qmm', 0, 40 + (m % 8)])
                        M['qcast', 1, m] = s_q8[m % 8].incd(eng.dma_start(
                            out=qkv_sb8[m % 8][:, :],
                            in_=wqkv_bf.ap()[m * P:(m + 1) * P, :]))
                    [SCA, SY][m % 2](_)
                    def _(g, m=m):
                        if m % 3 == 2 and M['nwob'] < 16:
                            sl = M['nwob']
                            M['wobf', sl] = s_wob.incd(g.dma_start(
                                out=wout_bf.ap()[sl * P:(sl + 1) * P, :],
                                in_=wout[sl * P:(sl + 1) * P, :]))
                            M['nwob'] += 1
                    GPS(_)
                def _(t, m=m, half=half, hb=hb):
                    W(t, M['qcast', half, m])
                    W(t, M['ytrd', 3 if half == 0 else 7])
                    if m >= 2:
                        W(t, M['qevac', half, m - 2])
                    elif half == 1:
                        W(t, M['attnA_oevac'])
                    ps = PS[m % 2].ap()[:, 0:512]
                    mm = None
                    for kc in range(16):
                        mm = t.matmul(ps, qkv_sb8[m % 8][:, kc * P:(kc + 1) * P],
                                      yT[:, kc * TOK + hb: kc * TOK + hb + 512],
                                      start=(kc == 0), stop=(kc == 15))
                    M['qmm', half, m] = tt_.inc(mm)
                TEN(_)
                def _(v, m=m, half=half):
                    W(v, M['qmm', half, m])
                    ps = PS[m % 2].ap()[:, 0:512]
                    store = [QTh, KTh, VTh][m // 16]
                    h = m % 16
                    dst = store.rearrange("p (g hj) -> p g hj", hj=P)[:, :, h * 8:h * 8 + 8]
                    M['qevac', half, m] = vv.inc(v.tensor_copy(dst, ps))
                VEC(_)
                drip(5)

        emit_qkv_half(0)

        psS4 = [ps2.ap()[:, 0:512], ps2.ap()[:, 512:1024], ps0.ap()[:, 0:512]]
        psDx = ps0.ap()[0:1, 512:1024]
        psRx = ps0.ap()[:, 512:1024]
        psV2 = [ps1.ap().bitcast(BF16)[:, 0:128], ps1.ap().bitcast(BF16)[:, 1024:1152]]
        psO2 = [ps3.ap()[:, 0:128], ps3.ap()[:, 512:640]]

        def emit_attn_half(half, ndrip):
            def pk(base, b, j):
                k = (base, half, b - j)
                if k in M:
                    return M[k]
                if half == 1:
                    return M.get((base, 0, 16 - j + b))
                return None

            def pkI(base, b, i, j=1):
                k = (base, half, b, i)
                if k in M:
                    return M[k]
                if half == 1:
                    return M.get((base, 0, 16 + b, i))
                return None

            for it in range(20):
                b = it
                def _(t, b=b, half=half):
                    if b >= 4:
                        W(t, M['recip', half, b - 4])
                        W(t, pkI('rsev', b - 5, 0))
                        M['rbc', half, b - 4] = tt_.inc(t.matmul(
                            psRx, oncF[0:1, :], rbufF, start=True, stop=True))
                    if b < 16:
                        W(t, M['qevac', half, 47])
                        W(t, pk('expdone', b, 3))
                        mm = None
                        for i in range(4):
                            g = 4 * b + i
                            mm = t.matmul(psS4[b % 3][:, i * P:(i + 1) * P],
                                          KTh[:, g * P:(g + 1) * P],
                                          QTh[:, g * P:(g + 1) * P],
                                          start=True, stop=True)
                        M['S', half, b] = tt_.inc(mm)
                    if 1 <= b <= 16:
                        for i in (2, 3):
                            g = 4 * (b - 1) + i
                            W(t, pkI('vevac', b - 1, i - 2))
                            M['trV', half, b - 1, i] = tt_.inc(t.transpose(
                                psV2[i % 2], VTh[:, g * P:(g + 1) * P], idc[:, :]))
                    if b >= 4:
                        b4 = b - 4
                        W(t, M['maskmul', half, b4])
                        for i in (0, 1):
                            W(t, M['vevac', half, b4, i])
                            W(t, pkI('oevac', b4 - 1, i + 2))
                            M['avdone', half, b4, i] = tt_.inc(t.matmul(
                                psO2[i % 2], Vs[i + 4 * (b4 % 4)],
                                Wm[b4 % 4][:, i * P:(i + 1) * P],
                                start=True, stop=True))
                TEN(_)
                if b < 16:
                    def _(s, b=b, half=half):
                        W(s, M['S', half, b])
                        W(s, pk('maskmul', b, 4))
                        W(s, M['eps'])
                        M['expdone', half, b] = ss.inc(s.activation(
                            Wt[b % 4], psS4[b % 3], AF.Exp, bias=nc20, scale=SC))
                    SCA(_)
                def _(v, b=b, half=half):
                    if 1 <= b <= 16:
                        for i in (2, 3):
                            W(v, M['trV', half, b - 1, i])
                            W(v, M.get(('avdone', half, b - 5, i)) or (
                                M.get(('avdone', 0, 11 + b, i)) if half == 1 else None))
                            M['vevac', half, b - 1, i] = vv.inc(v.tensor_copy(
                                Vs[i + 4 * ((b - 1) % 4)], psV2[i % 2]))
                    if b >= 4:
                        b4 = b - 4
                        W(v, M['rbc', half, b4])
                        W(v, pk('oevl', b4, 2))
                        M['rsev', half, b4, 0] = vv.inc(v.tensor_copy(rs[b4 % 2], psRx))
                        v.wait_ge(vv.h, vv.n)
                        for i in (0, 1):
                            g = 4 * b4 + i
                            W(v, M['avdone', half, b4, i])
                            dst = attnT.rearrange("p (h t) -> p h t", h=16)[
                                :, :, half * 512 + g * 8: half * 512 + g * 8 + 8]
                            M['oevac', half, b4, i] = vv.inc(v.tensor_tensor(
                                dst, psO2[i % 2], rs[b4 % 2][:, i * P:(i + 1) * P],
                                ALU.mult))
                VEC(_)
                def _(t, b=b, half=half):
                    if b < 16:
                        for i in (0, 1):
                            g = 4 * b + i
                            W(t, pkI('vevac', b - 1, i + 2))
                            M['trV', half, b, i] = tt_.inc(t.transpose(
                                psV2[i % 2], VTh[:, g * P:(g + 1) * P], idc[:, :]))
                    if 3 <= b < 19:
                        b3 = b - 3
                        W(t, M['maskmul', half, b3])
                        W(t, pkI('rsev', b3 - 1, 0))
                        M['denom', half, b3] = tt_.inc(t.matmul(
                            psDx, oncol[:, :], Wm[b3 % 4], start=True, stop=True))
                    if b >= 4:
                        b4 = b - 4
                        for i in (2, 3):
                            W(t, M['vevac', half, b4, i])
                            W(t, pkI('oevac', b4, i - 2))
                            M['avdone', half, b4, i] = tt_.inc(t.matmul(
                                psO2[i % 2], Vs[i + 4 * (b4 % 4)],
                                Wm[b4 % 4][:, i * P:(i + 1) * P],
                                start=True, stop=True))
                        M['avlast', half, b4] = M['avdone', half, b4, 3]
                TEN(_)
                if 3 <= b < 19:
                    def _(s, b=b, half=half):
                        b3 = b - 3
                        if b == 3 and half == 0:
                            W(s, M['ytrd', 7])
                        W(s, M['denom', half, b3])
                        W(s, pk('rbc', b3, 1))
                        ss.inc(s.activation(rb2F, psDx, AF.Ln))
                        s.wait_ge(ss.h, ss.n)
                        M['recip', half, b3] = ss.inc(s.activation(
                            rbufF, rb2F, AF.Exp, scale=-1.0))
                    SCA(_)
                def _(v, b=b, half=half):
                    if b < 16:
                        for i in (0, 1):
                            W(v, M['trV', half, b, i])
                            W(v, M.get(('avdone', half, b - 4, i)) or (
                                M.get(('avdone', 0, 12 + b, i)) if half == 1 else None))
                            M['vevac', half, b, i] = vv.inc(v.tensor_copy(
                                Vs[i + 4 * (b % 4)], psV2[i % 2]))
                        W(v, M['expdone', half, b])
                        W(v, pk('avlast', b, 4))
                        mrep = bass.AP(mkc.ap().tensor, 0, [[P, P], [0, 4], [1, P]])
                        M['maskmul', half, b] = vv.inc(
                            v.tensor_tensor(Wm[b % 4], Wt[b % 4], mrep, ALU.mult))
                    if b >= 4:
                        b4 = b - 4
                        W(v, M['rsev', half, b4, 0])
                        for i in (2, 3):
                            g = 4 * b4 + i
                            W(v, M['avdone', half, b4, i])
                            dst = attnT.rearrange("p (h t) -> p h t", h=16)[
                                :, :, half * 512 + g * 8: half * 512 + g * 8 + 8]
                            M['oevac', half, b4, i] = vv.inc(v.tensor_tensor(
                                dst, psO2[i % 2], rs[b4 % 2][:, i * P:(i + 1) * P],
                                ALU.mult))
                        M['oevl', half, b4] = M['oevac', half, b4, 3]
                VEC(_)
                if b < 16:
                    drip(ndrip)

        emit_attn_half(0, 1)
        M['attnA_end'] = tt_.now()
        M['attnA_oevac'] = vv.now()

        emit_qkv_half(1)

        emit_attn_half(1, 2)
        M['attnB_end'] = tt_.now()
        M['attnB_oevac'] = vv.now()

        def _(sc_):
            for kc in range(4):
                W(sc_, M['wobf', 15])
                W(sc_, M['qmm', 1, 44 + kc])
                M['wopre', kc] = s_qw[kc].incd(sc_.dma_start(
                    out=qkv_sb[kc][:, :], in_=wout_bf.ap()[kc * P:(kc + 1) * P, :]))
        SCA(_)

        def emit_cc(name, ins_t, outs_t, sem, wms):
            def _(g):
                W(g, wms)
                M['cc', name] = sem.inc(g.collective_compute(
                    "AllGather", ALU.bypass, replica_groups=[list(range(NCORES))],
                    ins=[ins_t.ap().opt()], outs=[outs_t.ap().opt()]))
            GPS(_)

        drip_finish('gate')
        emit_cc('gate', gTsh, wgT, s_ccg, M['rawdone', 'gate'])
        drip_finish('up')
        emit_cc('up', uTsh, wuT, s_ccu, M['rawdone', 'up'])

        groups8 = [ps0.ap()[:, 0:512], ps0.ap()[:, 512:1024],
                   ps1.ap()[:, 0:512], ps1.ap()[:, 512:1024],
                   ps2.ap()[:, 0:512], ps2.ap()[:, 512:1024],
                   ps3.ap()[:, 0:512], ps3.ap()[:, 512:1024]]
        LN2 = {tt: {k: slot(96 + 8 * tt + j) for j, k in enumerate(
            ['negm', 'lnv', 'rstd', 'negmr'])} for tt in range(NT)}
        LN3 = {tt: {k: slot(320 + 8 * tt + j) for j, k in enumerate(
            ['negm', 'lnv', 'rstd', 'negmr'])} for tt in range(NT)}
        LN4 = {tt: {k: slot(440 + 4 * tt + j) for j, k in enumerate(
            ['negm', 'lnv', 'rstd', 'negmr'])} for tt in range(NT)}

        for pair in range(4):
            tts = (2 * pair, 2 * pair + 1)
            order = list(range(16)) if pair % 2 == 0 else list(range(15, -1, -1))
            def _(sy, pair=pair, tts=tts):
                for tt in tts:
                    if tt >= 2:
                        W(sy, M['zbf', tt - 2])
                        W(sy, M['spill', tt - 2])
                    else:
                        W(sy, M['qmm', 1, 47])
                    M['x5', tt] = s_x[tt % 2].incd(sy.dma_start(
                        out=xrot[tt % 2][:, :], in_=x_in[tt * P:(tt + 1) * P, :]))
            SY(_)
            for oi, kc in enumerate(order):
                pre0 = (pair == 0 and kc < 4)
                needs_load = (kc >= 4) if pair == 0 else (oi >= 8)
                if needs_load:
                    def _(eng, pair=pair, kc=kc):
                        W(eng, M['wobf', 15])
                        if pair == 0 and kc < 12:
                            W(eng, M['attnB_end'])
                        k = ('opk', pair, kc ^ 8)
                        if k in M:
                            W(eng, M[k])
                        M['wo', pair, kc] = s_wo8[kc % 8].incd(
                            eng.dma_start(out=woutr[kc % 8][:, :],
                                          in_=wout_bf.ap()[kc * P:(kc + 1) * P, :]))
                    if kc % 2 == 0:
                        SCA(_)
                    else:
                        GPS(_)
                def _(t, pair=pair, kc=kc, oi=oi, tts=tts, pre0=pre0,
                      needs_load=needs_load):
                    if pre0:
                        W(t, M['wopre', kc])
                    elif needs_load:
                        W(t, M['wo', pair, kc])
                    if oi == 0:
                        W(t, M['attnB_end'])
                        W(t, M['attnB_oevac'])
                        if pair > 0:
                            W(t, M['s1evac', 2 * pair - 1])
                    src = qkv_sb[kc] if pre0 else woutr[kc % 8]
                    mm = None
                    for ti, tt in enumerate(tts):
                        for n4 in range(4):
                            mm = t.matmul(groups8[ti * 4 + n4],
                                          attnT[:, kc * TOK + tt * P: kc * TOK + (tt + 1) * P],
                                          src[:, n4 * 512:(n4 + 1) * 512],
                                          start=(oi == 0), stop=False)
                    M['opk', pair, kc] = tt_.inc(mm)
                TEN(_)

            def _(t, pair=pair, tts=tts, order=order):
                W(t, M['boutbf'])
                W(t, M['opk', pair, order[-1]])
                for ti, tt in enumerate(tts):
                    for n4 in range(4):
                        M['opd', tt, n4] = tt_.inc(t.matmul(
                            groups8[ti * 4 + n4], onc[0:1, :],
                            bout_b[0:1, n4 * 512:(n4 + 1) * 512], start=False, stop=True))
            TEN(_)
            for ti, tt in enumerate(tts):
                def _(v, tt=tt, ti=ti):
                    c = None
                    for n4 in range(4):
                        W(v, M['opd', tt, n4])
                        c = v.tensor_copy(s1b[ti][:, n4 * 512:(n4 + 1) * 512], groups8[ti * 4 + n4])
                    M['s1evac', tt] = vv.inc(c)
                VEC(_)
            def _(v, tts=tts):
                for ti, tt in enumerate(tts):
                    L2 = LN2[tt]
                    W(v, M['s1evac', tt])
                    b = None
                    for c4 in range(4):
                        b = v.bn_stats(bnst[:, c4 * 6:(c4 + 1) * 6],
                                       s1b[ti][:, c4 * 512:(c4 + 1) * 512])
                    vv.inc(b)
                    v.wait_ge(vv.h, vv.n)
                    vv.inc(v.bn_aggr(mv2t[tt], bnst))
                    v.wait_ge(vv.h, vv.n)
                    M['mv2', tt] = vv.inc(
                        v.tensor_scalar(L2['negm'], mv2t[tt][:, 0:1], -1.0, None, ALU.mult))
            VEC(_)
            def _(s, tts=tts):
                for tt in tts:
                    L2 = LN2[tt]
                    W(s, M['mv2', tt])
                    M['rstd2', tt] = ss.inc(s.activation(
                        L2['lnv'], mv2t[tt][:, 1:2], AF.Sqrt, bias=epsc, scale=1.0))
            SCA(_)
            def _(v, tts=tts):
                for ti, tt in enumerate(tts):
                    L2 = LN2[tt]
                    W(v, M['rstd2', tt])
                    vv.inc(v.reciprocal(L2['rstd'], L2['lnv']))
                    v.wait_ge(vv.h, vv.n)
                    vv.inc(v.tensor_scalar(L2['negmr'], L2['negm'], L2['rstd'], None, ALU.mult))
                    if tt >= 2:
                        W(v, M['r1', tt - 2])
                    v.wait_ge(vv.h, vv.n)
                    M['n2', tt] = vv.inc(v.tensor_scalar(
                        [r1rA, r1rB][tt % 2], s1b[ti], L2['rstd'], L2['negmr'],
                        ALU.mult, ALU.add))
                for tt in tts:
                    W(v, M['n2', tt])
                    W(v, M['x5', tt])
                    M['r1', tt] = vv.inc(v.tensor_tensor(
                        xrot[tt % 2][:, :], xrot[tt % 2][:, :],
                        [r1rA, r1rB][tt % 2], ALU.add))
            VEC(_)
            def _(v, tts=tts):
                for tt in tts:
                    L3 = LN3[tt]
                    W(v, M['r1', tt])
                    b = None
                    for c4 in range(4):
                        b = v.bn_stats(bnst[:, c4 * 6:(c4 + 1) * 6],
                                       xrot[tt % 2][:, c4 * 512:(c4 + 1) * 512])
                    vv.inc(b)
                    v.wait_ge(vv.h, vv.n)
                    vv.inc(v.bn_aggr(mv3t[tt], bnst))
                    v.wait_ge(vv.h, vv.n)
                    M['mv3', tt] = vv.inc(
                        v.tensor_scalar(L3['negm'], mv3t[tt][:, 0:1], -1.0, None, ALU.mult))
            VEC(_)
            def _(s, tts=tts):
                for tt in tts:
                    L3 = LN3[tt]
                    W(s, M['mv3', tt])
                    M['rstd3', tt] = ss.inc(s.activation(
                        L3['lnv'], mv3t[tt][:, 1:2], AF.Sqrt, bias=epsc, scale=1.0))
            SCA(_)
            def _(v, tts=tts):
                for tt in tts:
                    L3 = LN3[tt]
                    W(v, M['rstd3', tt])
                    vv.inc(v.reciprocal(L3['rstd'], L3['lnv']))
                    v.wait_ge(vv.h, vv.n)
                    vv.inc(v.tensor_scalar(L3['negmr'], L3['negm'], L3['rstd'], None, ALU.mult))
                    v.wait_ge(vv.h, vv.n)
                    M['zbf', tt] = vv.inc(v.tensor_scalar(
                        ztile[tt], xrot[tt % 2][:, :], L3['rstd'], L3['negmr'],
                        ALU.mult, ALU.add))
            VEC(_)
            def _(sy, tts=tts):
                for tt in tts:
                    W(sy, M['zbf', tt])
                    M['spill', tt] = [s_sp, s_sp2][tt % 2].incd(sy.dma_start(
                        out=r1d.ap()[tt * P:(tt + 1) * P, :], in_=xrot[tt % 2][:, :]))
            SY(_)
            drip(20)

        def _(t):
            W(t, M['s1evac', 7])
        TEN(_)
        for tt in range(NT):
            emit_tr16(M['zbf', tt], ztile[tt], zT, tt, 'ztr16')
        M['ztr_all'] = tr_evac[-1]

        if debug:
            def _(sy):
                W(sy, M['ytrd', 7])
                s_cst.incd(sy.dma_start(out=taps['tap_yT'], in_=yT))
                W(sy, M['attnB_oevac'])
                s_cst.incd(sy.dma_start(out=taps['tap_attnT'], in_=attnT))
                for tt in range(NT):
                    W(sy, M['spill', tt])
                s_cst.incd(sy.dma_start(out=taps['tap_r1'], in_=r1d.ap()))
                W(sy, M['ztr_all'])
                s_cst.incd(sy.dma_start(out=taps['tap_zT'], in_=zT))
                W(sy, M['cc', 'gate'])
                s_cst.incd(sy.dma_start(out=taps['tap_wg'], in_=wgT.ap()))
                W(sy, M['cc', 'down'])
                M['dbg1'] = s_cst.incd(sy.dma_start(out=taps['tap_wd'], in_=wdF.ap()))
            SY(_)

        for ot in range(NOT):
            gslot = [ps0, ps2][ot % 2]
            uslot = [ps1, ps3][ot % 2]
            def _(sy, ot=ot):
                if ot == 0:
                    W(sy, M['cc', 'gate'])
                    W(sy, M['cc', 'up'])
                    W(sy, M['n2', 7])
                    W(sy, M['opd', 7, 3])
                if ot >= 2:
                    W(sy, M['gumm', ot - 2])
                s_gu[ot % 2].incd(sy.dma_start(
                    out=gate_sb[ot % 2].rearrange("p (a c) -> p a c", c=P),
                    in_=wgT.ap().rearrange("(a p) o -> p a o", p=P)[:, :, ot * P:(ot + 1) * P]))
                M['guld', ot] = s_gu[ot % 2].incd(sy.dma_start(
                    out=up_sb[ot % 2].rearrange("p (a c) -> p a c", c=P),
                    in_=wuT.ap().rearrange("(a p) o -> p a o", p=P)[:, :, ot * P:(ot + 1) * P]))
            SY(_)
            def _(t, ot=ot, gslot=gslot, uslot=uslot):
                W(t, M['guld', ot])
                if ot == 0:
                    W(t, M['ztr_all'])
                if ot >= 2:
                    W(t, M['hact', ot - 2])
                mm = None
                for kc in range(16):
                    t.matmul(gslot.ap()[:, 0:512], gate_sb[ot % 2][:, kc * P:(kc + 1) * P],
                             zT[:, kc * TOK: kc * TOK + 512], start=(kc == 0), stop=(kc == 15))
                    t.matmul(gslot.ap()[:, 512:1024], gate_sb[ot % 2][:, kc * P:(kc + 1) * P],
                             zT[:, kc * TOK + 512: kc * TOK + 1024], start=(kc == 0),
                             stop=(kc == 15))
                for kc in range(16):
                    t.matmul(uslot.ap()[:, 0:512], up_sb[ot % 2][:, kc * P:(kc + 1) * P],
                             zT[:, kc * TOK: kc * TOK + 512], start=(kc == 0), stop=(kc == 15))
                    mm = t.matmul(uslot.ap()[:, 512:1024], up_sb[ot % 2][:, kc * P:(kc + 1) * P],
                                  zT[:, kc * TOK + 512: kc * TOK + 1024], start=(kc == 0),
                                  stop=(kc == 15))
                M['gumm', ot] = tt_.inc(mm)
            TEN(_)
            def _(s, ot=ot, gslot=gslot, uslot=uslot):
                W(s, M['gumm', ot])
                if ot >= 2:
                    W(s, M['hact', ot - 2])
                M['silu', ot] = ss.inc(s.activation(sgb2[ot % 2], gslot.ap()[:, :], AF.Silu))
            SCA(_)
            def _(v, ot=ot, uslot=uslot):
                W(v, M['silu', ot])
                if ot >= 2:
                    W(v, M['hact', ot - 2])
                vv.inc(v.tensor_copy(ubuf2[ot % 2], uslot.ap()[:, :]))
                v.wait_ge(vv.h, vv.n)
                M['hact', ot] = vv.inc(v.tensor_tensor(
                    h_act[:, ot * TOK:(ot + 1) * TOK], sgb2[ot % 2], ubuf2[ot % 2], ALU.mult))
            VEC(_)
            drip_down(6)

        while gdcursor[0] < len(gitems_d):
            drip_down(8)

        if debug:
            def _(sy):
                W(sy, M['hact', NOT - 1])
                M['dbg2'] = s_cst.incd(sy.dma_start(out=taps['tap_h'], in_=h_act))
            SY(_)

        dw_ld = []
        dw_mm = []

        def emit_p8_group(tg):
            for nqp in range(2):
                for kc in range(NOT):
                    def _(sy, tg=tg, nqp=nqp, kc=kc):
                        if len(dw_ld) < 2:
                            W(sy, M['cc', 'down'])
                            W(sy, M['gumm', NOT - 1])
                        if len(dw_mm) >= 4:
                            W(sy, dw_mm[-4])
                        sl = len(dw_ld) % 4
                        M['dwl', tg, nqp, kc] = s_dw[sl].incd(sy.dma_start(
                            out=wd_sb[sl][:, :],
                            in_=wdF.ap()[nqp * 1024:(nqp + 1) * 1024,
                                         kc * P:(kc + 1) * P],
                            transpose=True))
                        dw_ld.append(M['dwl', tg, nqp, kc])
                    SY(_)
                    def _(t, tg=tg, nqp=nqp, kc=kc):
                        sl = len(dw_mm) % 4
                        W(t, M['dwl', tg, nqp, kc])
                        if kc == 0:
                            if tg == 0 and nqp == 0:
                                W(t, M['hact', NOT - 1])
                                W(t, M['silu', NOT - 1])
                            elif nqp == 0:
                                W(t, M['devac', 0, 1, 3])
                            else:
                                W(t, M['devac', tg, 0, tg * 4 + 3])
                        mm = None
                        for j in range(4):
                            tt = tg * 4 + j
                            for q in range(2):
                                mm = t.matmul(groups8[j * 2 + q],
                                              h_act[:, kc * TOK + tt * P: kc * TOK + (tt + 1) * P],
                                              wd_sb[sl][:, q * 512:(q + 1) * 512],
                                              start=(kc == 0), stop=(kc == NOT - 1))
                        M['dwk', tg, nqp, kc] = tt_.inc(mm)
                        dw_mm.append(M['dwk', tg, nqp, kc])
                    TEN(_)
                def _(v, tg=tg, nqp=nqp):
                    W(v, M['dwk', tg, nqp, NOT - 1])
                    if tg == 0 and nqp == 0:
                        W(v, M['rawdone', 'down'])
                    if tg == 1 and nqp == 0:
                        W(v, M['spill', 6])
                        W(v, M['spill', 7])
                    for j in range(4):
                        tt = tg * 4 + j
                        for q in range(2):
                            nq = nqp * 2 + q
                            M['devac', tg, nqp, tt] = vv.inc(v.tensor_copy(
                                d_bf[:, tt * 2048 + nq * 512: tt * 2048 + (nq + 1) * 512],
                                groups8[j * 2 + q]))
                            v.wait_ge(vv.h, vv.n)
                            M['dst', nq, tt] = vv.inc(v.bn_stats(
                                stt[:, 256 + tt * 24 + nq * 6: 256 + tt * 24 + (nq + 1) * 6],
                                d_bf[:, tt * 2048 + nq * 512: tt * 2048 + (nq + 1) * 512]))
                VEC(_)

        def emit_p9_tile(tt):
            L4 = LN4[tt]
            def _(sc_, tt=tt):
                W(sc_, M['spill', tt])
                if tt < 2:
                    W(sc_, M['r1', 7])
                    W(sc_, M['zbf', 7])
                    W(sc_, M['spill', 7])
                else:
                    W(sc_, M['outadd', tt - 2])
                M['r1re', tt] = s_gu[tt % 2].incd(sc_.dma_start(
                    out=r1ld[tt % 2][:, :], in_=r1d.ap()[tt * P:(tt + 1) * P, :]))
            SCA(_)
            def _(v, tt=tt, L4=L4):
                W(v, M['dst', 3, tt])
                vv.inc(v.bn_aggr(mv4t[tt], stt[:, 256 + tt * 24: 256 + (tt + 1) * 24]))
                v.wait_ge(vv.h, vv.n)
                M['mv4', tt] = vv.inc(
                    v.tensor_scalar(L4['negm'], mv4t[tt][:, 0:1], -1.0, None, ALU.mult))
            VEC(_)
            def _(s, tt=tt, L4=L4):
                W(s, M['mv4', tt])
                M['rstd4', tt] = ss.inc(s.activation(
                    L4['lnv'], mv4t[tt][:, 1:2], AF.Sqrt, bias=epsc, scale=1.0))
            SCA(_)
            def _(v, tt=tt, L4=L4):
                W(v, M['rstd4', tt])
                vv.inc(v.reciprocal(L4['rstd'], L4['lnv']))
                v.wait_ge(vv.h, vv.n)
                vv.inc(v.tensor_scalar(L4['negmr'], L4['negm'], L4['rstd'], None, ALU.mult))
                if tt >= 1:
                    W(v, M['outadd', tt - 1])
                v.wait_ge(vv.h, vv.n)
                M['n4', tt] = vv.inc(v.tensor_scalar(
                    n4buf[:, 0:2048], d_bf[:, tt * 2048:(tt + 1) * 2048],
                    L4['rstd'], L4['negmr'], ALU.mult, ALU.add))
            VEC(_)
            def _(v, tt=tt):
                W(v, M['n4', tt])
                W(v, M['r1re', tt])
                if tt >= 2:
                    W(v, M['out', tt - 2])
                M['outadd', tt] = vv.inc(v.tensor_tensor(
                    outbufs[tt % 2][:, 0:2048], r1ld[tt % 2][:, :], n4buf[:, 0:2048], ALU.add))
            VEC(_)
            def _(sc_, tt=tt):
                W(sc_, M['outadd', tt])
                M['out', tt] = s_out[tt % 2].incd(sc_.dma_start(
                    out=y_out[tt * P:(tt + 1) * P, :], in_=outbufs[tt % 2][:, 0:2048]))
            SCA(_)

        emit_p8_group(0)
        for tt in range(4):
            emit_p9_tile(tt)
        emit_p8_group(1)
        for tt in range(4, NT):
            emit_p9_tile(tt)

        def _(sy):
            W(sy, M['out', NT - 2])
            W(sy, M['out', NT - 1])
            if debug:
                W(sy, M.get('dbg1'))
                W(sy, M.get('dbg2'))
        SY(_)

    return nc


_NC_CACHE = {}


def _get_nc(debug=False):
    if debug not in _NC_CACHE:
        _NC_CACHE[debug] = build_nc(debug)
    return _NC_CACHE[debug]


def make_in_maps(L0_attn_x, L0_attn_W_qkv, L0_attn_W_out, L0_attn_b_out,
                 L0_ffn_lut, L0_ffn_gate_idx, L0_ffn_up_idx, L0_ffn_down_idx):
    x = np.ascontiguousarray(np.asarray(L0_attn_x, dtype=np.float32).reshape(8192, D))
    wq0 = np.asarray(L0_attn_W_qkv, dtype=np.float32)
    wq = np.ascontiguousarray(
        wq0.reshape(16, 128, 48, 128).transpose(2, 1, 0, 3).reshape(6144, 2048))
    wo = np.ascontiguousarray(np.asarray(L0_attn_W_out, dtype=np.float32))
    bo = np.ascontiguousarray(np.asarray(L0_attn_b_out, dtype=np.float32))
    lu = np.ascontiguousarray(np.asarray(L0_ffn_lut, dtype=np.float32))
    gi = np.asarray(L0_ffn_gate_idx, dtype=np.int32)
    ui = np.asarray(L0_ffn_up_idx, dtype=np.int32)
    di = np.asarray(L0_ffn_down_idx, dtype=np.int32)
    gi = np.ascontiguousarray(gi).view(np.uint8).reshape(gi.shape[0], gi.shape[1], 4)[:, :, 0].astype(np.int8)
    ui = np.ascontiguousarray(ui).view(np.uint8).reshape(ui.shape[0], ui.shape[1], 4)[:, :, 0].astype(np.int8)
    di = np.ascontiguousarray(di).view(np.uint8).reshape(di.shape[0], di.shape[1], 4)[:, :, 0].astype(np.int8)
    ident = np.eye(P, dtype=np.float32).astype(bfnp)
    mask = (np.arange(P)[:, None] % 8 == np.arange(P)[None, :] % 8).astype(bfnp)
    ones = np.ones((1, P), dtype=np.float32).astype(bfnp)
    maps = []
    for c in range(NCORES):
        maps.append(dict(
            x_in=np.ascontiguousarray(x[c * TOK:(c + 1) * TOK]),
            wqkv_my=np.ascontiguousarray(wq[c * 768:(c + 1) * 768]),
            wout=wo, bout=bo, lut=lu,
            gidx=np.ascontiguousarray(gi[:, c * SHC:(c + 1) * SHC].T),
            uidx=np.ascontiguousarray(ui[:, c * SHC:(c + 1) * SHC].T),
            didx=np.ascontiguousarray(di[c * SHC:(c + 1) * SHC, :]),
            identc=ident, maskc=mask, onesc=ones,
        ))
    return maps


def kernel(L0_attn_x, L0_attn_W_qkv, L0_attn_W_out, L0_attn_b_out,
           L0_ffn_lut, L0_ffn_gate_idx, L0_ffn_up_idx, L0_ffn_down_idx):
    nc = _get_nc(False)
    maps = make_in_maps(L0_attn_x, L0_attn_W_qkv, L0_attn_W_out, L0_attn_b_out,
                        L0_ffn_lut, L0_ffn_gate_idx, L0_ffn_up_idx, L0_ffn_down_idx)
    res = run_bass_kernel_spmd(nc, maps, core_ids=list(range(NCORES)))
    out = np.concatenate([np.asarray(r["y_out"]) for r in res.results], axis=0)
    return out.reshape(4, 2048, 2048).astype(np.float32)


# version 55 (final)
# speedup vs baseline: 1.1285x; 1.1285x over previous
import numpy as np
import ml_dtypes
import concourse.bass as bass
import concourse.mybir as mybir
from concourse.bass_utils import run_bass_kernel_spmd
from contextlib import ExitStack

F32 = mybir.dt.float32
BF16 = mybir.dt.bfloat16
I32 = mybir.dt.int32
I8 = mybir.dt.int8
AF = mybir.ActivationFunctionType
ALU = mybir.AluOpType
AX = mybir.AxisListType
bfnp = ml_dtypes.bfloat16

P = 128
D = 2048
TOK = 1024
NT = 8
NH = 16
FEAT3 = 6144
NI = 5504
NOT = 43
SHC = 256
SC = 0.08838834764831845
EPS = 1e-5
NCORES = 8


class SemC:
    def __init__(self, stack, nc, name):
        self.h = stack.enter_context(nc.semaphore(name))
        self.n = 0

    def inc(self, inst, k=1):
        inst.then_inc(self.h, k)
        self.n += k
        return (self, self.n)

    def incd(self, inst):
        return self.inc(inst, 16)

    def now(self):
        return (self, self.n)


def W(eng, ms):
    if not ms:
        return
    if isinstance(ms, list):
        for m in ms:
            W(eng, m)
        return
    eng.wait_ge(ms[0].h, ms[1])


def build_nc(debug=False):
    nc = bass.Bass("TRN2", target_bir_lowering=False)

    x_in = nc.dram_tensor("x_in", [TOK, D], F32, kind="ExternalInput").ap()
    wqkv_my = nc.dram_tensor("wqkv_my", [FEAT3 // NCORES, D], F32,
                             kind="ExternalInput").ap()
    wout = nc.dram_tensor("wout", [D, D], F32, kind="ExternalInput").ap()
    bout = nc.dram_tensor("bout", [D], F32, kind="ExternalInput").ap()
    lut = nc.dram_tensor("lut", [16], F32, kind="ExternalInput").ap()
    gidx = nc.dram_tensor("gidx", [SHC, NI], I8, kind="ExternalInput").ap()
    uidx = nc.dram_tensor("uidx", [SHC, NI], I8, kind="ExternalInput").ap()
    didx = nc.dram_tensor("didx", [SHC, NI], I8, kind="ExternalInput").ap()
    identc = nc.dram_tensor("identc", [P, P], BF16, kind="ExternalInput").ap()
    maskc = nc.dram_tensor("maskc", [P, P], BF16, kind="ExternalInput").ap()
    onesc = nc.dram_tensor("onesc", [1, P], BF16, kind="ExternalInput").ap()

    y_out = nc.dram_tensor("y_out", [TOK, D], F32, kind="ExternalOutput").ap()
    taps = {}
    if debug:
        for nm, shp, dt in [
            ("tap_yT", [P, NH * TOK], BF16), ("tap_attnT", [P, NH * TOK], BF16),
            ("tap_r1", [TOK, D], F32), ("tap_zT", [P, NH * TOK], BF16),
            ("tap_wg", [D, NI], BF16), ("tap_wd", [D, NI], BF16),
            ("tap_h", [P, NOT * TOK], BF16),
        ]:
            taps[nm] = nc.dram_tensor(nm, shp, dt, kind="ExternalOutput").ap()

    draw = nc.dram_tensor("draw_t", [SHC, NI], BF16)
    gTsh = nc.dram_tensor("gTsh", [SHC, NI], BF16)
    uTsh = nc.dram_tensor("uTsh", [SHC, NI], BF16)
    wgT = nc.dram_tensor("wgT", [D, NI], BF16, addr_space="Shared")
    wuT = nc.dram_tensor("wuT", [D, NI], BF16, addr_space="Shared")
    wdF = nc.dram_tensor("wdF", [D, NI], BF16, addr_space="Shared")
    wout_bf = nc.dram_tensor("wout_bf", [D, D], BF16)
    gqkv = nc.dram_tensor("gqkv", [FEAT3 // NCORES, D], BF16)
    wqkv_bf = nc.dram_tensor("wqkv_bf", [FEAT3, D], BF16, addr_space="Shared")
    r1d = nc.dram_tensor("r1d", [TOK, D], F32)

    st = ExitStack()
    e = st.enter_context
    with st:
        AR1 = e(nc.sbuf_tensor("AR1", [P, 44032], BF16))
        AR3 = e(nc.sbuf_tensor("AR3", [P, 16384], BF16))
        AR4 = e(nc.sbuf_tensor("AR4", [P, 19200], BF16))
        R1R = e(nc.sbuf_tensor("R1R", [P, 4096], F32))
        STRM = e(nc.sbuf_tensor("STRM", [P, 4096], BF16))
        SML = e(nc.sbuf_tensor("SML", [P, 8192], BF16))
        S1R = e(nc.sbuf_tensor("S1R", [P, 4096], BF16))
        STT = e(nc.sbuf_tensor("STT", [P, 512], F32))
        idc = e(nc.sbuf_tensor("idc", [P, P], BF16))
        mkc = e(nc.sbuf_tensor("mkc", [P, P], BF16))
        onc = e(nc.sbuf_tensor("onc", [1, P], BF16))
        oncol = e(nc.sbuf_tensor("oncol", [P, 1], BF16))
        oncF = e(nc.sbuf_tensor("oncF", [1, P], F32))
        lutb = e(nc.sbuf_tensor("lutb", [P, 16], F32))

        ps0 = e(nc.psum_tensor("ps0", [P, 1024], F32))
        ps1 = e(nc.psum_tensor("ps1", [P, 1024], F32))
        ps2 = e(nc.psum_tensor("ps2", [P, 1024], F32))
        ps3 = e(nc.psum_tensor("ps3", [P, 1024], F32))
        PS = [ps0, ps1, ps2, ps3]

        bout_b = AR1.ap()[0:1, 40960:43008]
        QTh = AR1.ap()[:, 0:8192]
        KTh = AR1.ap()[:, 8192:16384]
        VTh = AR1.ap()[:, 16384:24576]
        attnT = AR1.ap()[:, 24576:40960]
        h_act = AR1.ap()[:, 0:44032]
        n4buf = AR3.ap()[:, 0:4096].bitcast(F32)
        outbufs = [AR3.ap()[:, 4096 + i * 4096: 4096 + (i + 1) * 4096].bitcast(F32)
                   for i in range(2)]

        yT = AR3.ap()[:, 0:16384]
        zT = AR3.ap()[:, 0:16384]

        idx_i32 = AR4.ap()[:, 0:640].bitcast(I8)
        idxbf = AR4.ap()[:, 2560:3840]
        gtmp = AR4.ap()[:, 3840:5120]
        accsl = [AR4.ap()[:, 5120 + i * 1280: 5120 + (i + 1) * 1280] for i in range(2)]
        xrot = [AR4.ap()[:, 11008 + i * 4096: 11008 + (i + 1) * 4096].bitcast(F32)
                for i in range(2)]
        d_bf = AR4.ap()[:, 0:16384]

        r1rA = R1R.ap()[:, 0:2048]
        r1rB = R1R.ap()[:, 2048:4096]
        r1ld = [r1rA, r1rB]

        qkv_sb = [STRM.ap()[:, 0:2048], STRM.ap()[:, 2048:4096],
                  S1R.ap()[:, 0:2048], S1R.ap()[:, 2048:4096]]
        gate_sb = [STRM.ap()[:, i * 2048:(i + 1) * 2048] for i in range(2)]
        qkv_ext = [AR4.ap()[:, 11008 + i * 2048: 11008 + (i + 1) * 2048]
                   for i in range(4)]
        woutr = [AR1.ap()[:, i * 2048:(i + 1) * 2048] for i in range(8)]
        ztile = ([AR4.ap()[:, i * 2048:(i + 1) * 2048] for i in range(5)] +
                 [AR1.ap()[:, 16384 + i * 2048: 16384 + (i + 1) * 2048] for i in range(3)])
        wd_sb = [STRM.ap()[:, i * 1024:(i + 1) * 1024] for i in range(4)]
        up_sb = [S1R.ap()[:, i * 2048:(i + 1) * 2048] for i in range(2)]
        s1b = [S1R.ap()[:, i * 2048:(i + 1) * 2048] for i in range(2)]

        Wt = [SML.ap()[:, i * 512:(i + 1) * 512] for i in range(4)]
        Wm = [SML.ap()[:, 2048 + i * 512:2048 + (i + 1) * 512] for i in range(4)]
        rs = [AR4.ap()[:, 7680 + i * 512:7680 + (i + 1) * 512] for i in range(2)]
        Vs = [AR4.ap()[:, 8704 + i * 128:8704 + (i + 1) * 128] for i in range(16)]
        rbufF = SML.ap()[0:1, 4096:5120].bitcast(F32)
        rb2F = SML.ap()[0:1, 5120:7168].bitcast(F32)[:, 0:512]
        zbuf = SML.ap()[:, 0:2048]
        sgb2 = [SML.ap()[:, 2048:3072], SML.ap()[:, 4096:5120]]
        ubuf2 = [SML.ap()[:, 3072:4096], SML.ap()[:, 5120:6144]]
        ybuf = [SML.ap()[:, 4096 + i * 2048: 4096 + (i + 1) * 2048] for i in range(2)]
        stag2 = SML.ap()[:, 5120:7872]

        stt = STT.ap()
        epsc = stt[:, 0:1]

        def slot(i):
            return stt[:, 16 + i:17 + i]
        nc20 = stt[:, 4:5]
        bnst = stt[:, 224:248]
        mvt = stt[:, 248:250]
        mv2t = {tt: stt[:, 176 + 2 * tt: 178 + 2 * tt] for tt in range(NT)}
        mv3t = {tt: stt[:, 400 + 2 * tt: 402 + 2 * tt] for tt in range(NT)}
        mv4t = {tt: stt[:, 96 + 2 * tt: 98 + 2 * tt] for tt in range(NT)}

        vv = SemC(st, nc, "vv")
        ss = SemC(st, nc, "ss")
        tt_ = SemC(st, nc, "tts")
        gg = SemC(st, nc, "gg")
        s_cst = SemC(st, nc, "scst")
        s_idx = SemC(st, nc, "sidx")
        s_tsh = SemC(st, nc, "stsh")
        s_wo = [SemC(st, nc, "swo0"), SemC(st, nc, "swo1")]
        s_wo2 = [SemC(st, nc, "swo2"), SemC(st, nc, "swo3")]
        s_sp = SemC(st, nc, "ssp")
        s_sp2 = SemC(st, nc, "ssp2")
        s_zt = [SemC(st, nc, "szt0"), SemC(st, nc, "szt1"),
                SemC(st, nc, "szt2"), SemC(st, nc, "szt3")]
        s_gu = [SemC(st, nc, "sgu0"), SemC(st, nc, "sgu1")]
        s_dw = [SemC(st, nc, "sdw0"), SemC(st, nc, "sdw1"),
                SemC(st, nc, "sdw2"), SemC(st, nc, "sdw3")]
        s_out = [SemC(st, nc, "sout0"), SemC(st, nc, "sout1")]
        s_x = [SemC(st, nc, "sx0"), SemC(st, nc, "sx1")]
        s_qc = [SemC(st, nc, "sqc0"), SemC(st, nc, "sqc1")]
        s_qs = [SemC(st, nc, "sqs0"), SemC(st, nc, "sqs1")]
        s_qw = [SemC(st, nc, "sqw0"), SemC(st, nc, "sqw1"),
                SemC(st, nc, "sqw2"), SemC(st, nc, "sqw3")]
        s_q4 = None
        s_bn = [SemC(st, nc, "sbn0"), SemC(st, nc, "sbn1")]
        s_wob = SemC(st, nc, "swob")
        s_bb = SemC(st, nc, "sbb")
        s_ccq = SemC(st, nc, "sccq")
        s_ccg = SemC(st, nc, "sccg")
        s_ccu = SemC(st, nc, "sccu")
        s_ccd = SemC(st, nc, "sccd")
        s_rd = [SemC(st, nc, "srd0"), SemC(st, nc, "srd1")]
        s_st4 = SemC(st, nc, "sstg4")

        block = e(nc.Block())

        def SY(f): block.sync(f)
        def SCA(f): block.scalar(f)
        def VEC(f): block.vector(f)
        def TEN(f): block.tensor(f)
        def GPS(f): block.gpsimd(f)

        s_qA = [s_qc[0], s_qc[1], SemC(st, nc, "sqc2"), SemC(st, nc, "sqc3")]
        s_qB = [s_qs[0], s_qs[1], SemC(st, nc, "sqs2"), SemC(st, nc, "sqs3")]
        s_stg5 = s_qB + [s_st4]
        s_q8 = s_qA + s_qB
        qkv_sb8 = qkv_sb + qkv_ext
        s_wo8 = [s_wo[0], s_wo[1], s_wo2[0], s_wo2[1],
                 SemC(st, nc, "swo4"), SemC(st, nc, "swo5"),
                 SemC(st, nc, "swo6"), SemC(st, nc, "swo7")]

        M = {}

        def _(sy):
            s_cst.incd(sy.dma_start(out=idc[:, :], in_=identc))
            s_cst.incd(sy.dma_start(out=mkc[:, :], in_=maskc))
            s_cst.incd(sy.dma_start(out=onc[:1, :], in_=onesc))
            M['const'] = s_cst.incd(sy.dma_start(
                out=lutb[:, :], in_=lut[None, :].to_broadcast([P, 16])))
        SY(_)

        def _(g):
            g.memset(epsc, EPS)
            g.memset(nc20, -20.0)
            g.memset(oncol[:, :], 1.0)
            g.memset(oncF[:, :], 1.0)
            M['eps'] = gg.inc(g.engine_nop())
            M['boutbf'] = s_bb.incd(g.dma_start(out=bout_b, in_=bout[None, :]))
        GPS(_)
        M['nwob'] = 0

        for j in range(6):
            def _(g, j=j):
                if j >= 3:
                    W(g, M['cvs', j - 3])
                M['cvl', j] = s_qA[j % 3].incd(g.dma_start(
                    out=qkv_sb[j % 3][:, :], in_=wqkv_my[j * P:(j + 1) * P, :]))
            GPS(_)
            def _(sy, j=j):
                W(sy, M['cvl', j])
                M['cvs', j] = s_qw[j % 3].incd(sy.dma_start(
                    out=gqkv.ap()[j * P:(j + 1) * P, :], in_=qkv_sb[j % 3][:, :]))
                M['convlast', j % 3] = M['cvs', j]
            SY(_)
        def _(g):
            for j in range(6):
                W(g, M['cvs', j])
            M['cc', 'qkv'] = s_ccq.inc(g.collective_compute(
                "AllGather", ALU.bypass, replica_groups=[list(range(NCORES))],
                ins=[gqkv.ap().opt()], outs=[wqkv_bf.ap().opt()]))
        GPS(_)

        xsl = [xrot[0], xrot[1], r1rA, r1rB]
        stg5 = xsl + [SML.ap()[:, 4096:8192].bitcast(F32)]
        s_x4 = s_x + [s_sp, s_tsh]

        def _(s):
            for t0 in range(4):
                M['x', t0] = s_x4[t0].incd(s.dma_start(
                    out=xsl[t0][:, :], in_=x_in[t0 * P:(t0 + 1) * P, :]))
        SCA(_)

        LN1 = {tt: {k: slot(10 * tt + j) for j, k in enumerate(
            ['negm', 'lnv', 'mv0', 'mv1', 'rstd', 'negmr'])} for tt in range(NT)}

        tr_evac = []
        psbf_tr = [ps2.ap().bitcast(BF16)[:, 0:128],
                   ps3.ap().bitcast(BF16)[:, 0:128],
                   ps2.ap().bitcast(BF16)[:, 1024:1152],
                   ps3.ap().bitcast(BF16)[:, 1024:1152]]

        def emit_tr16(src_ms, srcbuf, dst, tt, mkey):
            for kc in range(16):
                def _(t, kc=kc):
                    if kc == 0:
                        W(t, src_ms)
                        W(t, M['const'])
                    if len(tr_evac) >= 4:
                        W(t, tr_evac[-4])
                    M[mkey, tt, kc] = tt_.inc(
                        t.transpose(psbf_tr[kc % 4], srcbuf[:, kc * P:(kc + 1) * P], idc[:, :]))
                TEN(_)
                def _(v, kc=kc):
                    W(v, M[mkey, tt, kc])
                    tr_evac.append(vv.inc(v.tensor_copy(
                        dst[:, kc * TOK + tt * P: kc * TOK + (tt + 1) * P], psbf_tr[kc % 4])))
                VEC(_)
            M[mkey + 'd', tt] = tr_evac[-1]

        gchunks = [(0, 5), (5, 5), (10, 5), (15, 5), (20, 5), (25, 5), (30, 5), (35, 5), (40, 3)]
        dchunks = [(0, 1280), (1280, 1280), (2560, 1280), (3840, 1280), (5120, 384)]
        gidx3 = gidx.rearrange("(a p) c -> p a c", p=P)
        uidx3 = uidx.rearrange("(a p) c -> p a c", p=P)
        didx3 = didx.rearrange("(a p) c -> p a c", p=P)
        M['idxcast_last'] = None
        M['accguard', 0] = None
        M['accguard', 1] = None

        gitems = []
        gcursor = [0]
        gmark = {}

        def _mk_idx_load(name, src, ci):
            def f(g):
                W(g, M['idxcast_last'])
                M['idx', name, ci] = s_idx.incd(g.dma_start(out=src[0], in_=src[1]))
            return ('gps', f)

        def _mk_cast(name, ci, flen, slotpar):
            def f(v):
                W(v, M['idx', name, ci])
                W(v, M['const'])
                M['idxcast_last'] = vv.inc(v.tensor_copy(idxbf[:, 0:flen], idx_i32[:, 0:flen]))
                W(v, M['accguard', slotpar])
                v.wait_ge(vv.h, vv.n)
                vv.inc(v.tensor_scalar(accsl[slotpar][:, 0:flen], idxbf[:, 0:flen], 0.0,
                                       lutb[:, 0:1], ALU.is_equal, ALU.mult))
            return ('vec', f)

        def _mk_kstep(name, ci, flen, slotpar, k):
            def f(v):
                v.wait_ge(vv.h, vv.n)
                vv.inc(v.tensor_scalar(gtmp[:, 0:flen], idxbf[:, 0:flen], float(k),
                                       lutb[:, k:k + 1], ALU.is_equal, ALU.mult))
                v.wait_ge(vv.h, vv.n)
                vv.inc(v.tensor_tensor(accsl[slotpar][:, 0:flen],
                                       accsl[slotpar][:, 0:flen], gtmp[:, 0:flen], ALU.add))
                if k == 15:
                    M['gch', name, ci] = (vv, vv.n)
            return ('vec', f)

        def _mk_bounce(name, ci, dst, flen, slotpar, srcview):
            def f(g):
                W(g, M['gch', name, ci])
                M['bnc', name, ci] = s_bn[slotpar].incd(g.dma_start(
                    out=dst, in_=srcview))
                M['accguard', slotpar] = M['bnc', name, ci]
            return ('gps', f)

        _gctr = [0]

        def _mk_matrix_items(name, src3d, dst_t):
            seq = []
            for mt in range(2):
                for q, (a0, ln) in enumerate(dchunks):
                    ci = mt * 5 + q
                    _gctr[0] += 1
                    sp = _gctr[0] % 2
                    seq.append(_mk_idx_load(name, (
                        idx_i32[:, 0:ln], src3d[:, mt, a0:a0 + ln]), ci))
                    seq.append(_mk_cast(name, ci, ln, sp))
                    for k in range(1, 16):
                        seq.append(_mk_kstep(name, ci, ln, sp, k))
                    seq.append(_mk_bounce(name, ci,
                                          dst_t.ap()[mt * P:(mt + 1) * P, a0:a0 + ln],
                                          ln, sp, accsl[sp][:, 0:ln]))
            seq.append(('mark', name))
            return seq

        gitems.extend(_mk_matrix_items('gate', gidx3, gTsh))
        gitems.extend(_mk_matrix_items('up', uidx3, uTsh))
        gitems_d = _mk_matrix_items('down', didx3, draw)
        gdcursor = [0]

        def drip(nvec):
            done = 0
            while gcursor[0] < len(gitems) and done < nvec:
                kind, fn = gitems[gcursor[0]]
                if kind == 'gps':
                    GPS(fn)
                elif kind == 'vec':
                    VEC(fn)
                    done += 1
                else:
                    M['rawdone', fn] = [s_bn[0].now(), s_bn[1].now()]
                gcursor[0] += 1

        def drip_finish(name):
            while gcursor[0] < len(gitems):
                kind, fn = gitems[gcursor[0]]
                if kind == 'mark':
                    M['rawdone', fn] = [s_bn[0].now(), s_bn[1].now()]
                    gcursor[0] += 1
                    if fn == name:
                        return
                    continue
                if kind == 'gps':
                    GPS(fn)
                else:
                    VEC(fn)
                gcursor[0] += 1

        def drip_down(nvec):
            done = 0
            while gdcursor[0] < len(gitems_d) and done < nvec:
                kind, fn = gitems_d[gdcursor[0]]
                if kind == 'gps':
                    if gdcursor[0] == 0:
                        def f0(g, fn=fn):
                            W(g, M['ztr_all'])
                            fn(g)
                        GPS(f0)
                    else:
                        GPS(fn)
                elif kind == 'vec':
                    VEC(fn)
                    done += 1
                else:
                    M['rawdone', fn] = [s_bn[0].now(), s_bn[1].now()]
                    emit_cc('down', draw, wdF, s_ccd, M['rawdone', 'down'])
                gdcursor[0] += 1

        drip(8)

        def ln1_stats(tt):
            def _(v, tt=tt):
                L = LN1[tt]
                lmv = stt[:, 16 + 10 * tt + 2: 16 + 10 * tt + 4]
                W(v, M['x', tt])
                b = None
                for c4 in range(4):
                    b = v.bn_stats(bnst[:, c4 * 6:(c4 + 1) * 6],
                                   xsl[tt % 4][:, c4 * 512:(c4 + 1) * 512])
                vv.inc(b)
                v.wait_ge(vv.h, vv.n)
                vv.inc(v.bn_aggr(lmv, bnst))
                v.wait_ge(vv.h, vv.n)
                M['ln1s', tt] = vv.inc(
                    v.tensor_scalar(L['negm'], lmv[:, 0:1], -1.0, None, ALU.mult))
            VEC(_)
            def _(s, tt=tt):
                L = LN1[tt]
                lmv = stt[:, 16 + 10 * tt + 2: 16 + 10 * tt + 4]
                W(s, M['ln1s', tt])
                W(s, M['eps'])
                M['rstd1', tt] = ss.inc(
                    s.activation(L['lnv'], lmv[:, 1:2], AF.Sqrt, bias=epsc, scale=1.0))
            SCA(_)

        for tt in range(4):
            ln1_stats(tt)
        for tt in range(NT):
            def _(v, tt=tt):
                L = LN1[tt]
                W(v, M['rstd1', tt])
                vv.inc(v.reciprocal(L['rstd'], L['lnv']))
                v.wait_ge(vv.h, vv.n)
                vv.inc(v.tensor_scalar(L['negmr'], L['negm'], L['rstd'], None, ALU.mult))
                if tt >= 2:
                    W(v, M['ytr', tt - 2, 15])
                v.wait_ge(vv.h, vv.n)
                M['ybf', tt] = vv.inc(v.tensor_scalar(
                    ybuf[tt % 2], xsl[tt % 4][:, :], L['rstd'], L['negmr'],
                    ALU.mult, ALU.add))
            VEC(_)
            def _(s, tt=tt):
                if tt + 4 < NT:
                    W(s, M['ln1s', tt])
                    W(s, M['ybf', tt])
                    M['x', tt + 4] = s_x4[tt % 4].incd(s.dma_start(
                        out=xsl[tt % 4][:, :], in_=x_in[(tt + 4) * P:(tt + 5) * P, :]))
            SCA(_)
            emit_tr16(M['ybf', tt], ybuf[tt % 2], yT, tt, 'ytr')
            if tt + 4 < NT:
                ln1_stats(tt + 4)
            drip(6)


        def emit_qkv_half(half):
            hb = half * 512
            for m in range(48):
                slot_eng = [GPS, GPS, GPS, SY, SY, SY, SCA, SCA]
                if half == 0:
                    def _(eng, m=m):
                        W(eng, M['cc', 'qkv'])
                        if m >= 8:
                            W(eng, M['qmm', 0, m - 8])
                        elif m < 3:
                            W(eng, M['convlast', m % 3])
                        elif m == 3:
                            pass
                        elif m < 6:
                            W(eng, M['ybf', 4])
                        else:
                            W(eng, M['ybf', 5])
                        M['qcast', 0, m] = s_q8[m % 8].incd(eng.dma_start(
                            out=qkv_sb8[m % 8][:, :],
                            in_=wqkv_bf.ap()[m * P:(m + 1) * P, :]))
                    slot_eng[m % 8](_)
                else:
                    def _(eng, m=m):
                        if m >= 8:
                            W(eng, M['qmm', 1, m - 8])
                        else:
                            W(eng, M['qmm', 0, 40 + (m % 8)])
                        M['qcast', 1, m] = s_q8[m % 8].incd(eng.dma_start(
                            out=qkv_sb8[m % 8][:, :],
                            in_=wqkv_bf.ap()[m * P:(m + 1) * P, :]))
                    slot_eng[m % 8](_)
                    def _(g, m=m):
                        if m % 3 == 2 and M['nwob'] < 16:
                            sl = M['nwob']
                            M['wobf', sl] = s_wob.incd(g.dma_start(
                                out=wout_bf.ap()[sl * P:(sl + 1) * P, :],
                                in_=wout[sl * P:(sl + 1) * P, :]))
                            M['nwob'] += 1
                    GPS(_)
                def _(t, m=m, half=half, hb=hb):
                    W(t, M['qcast', half, m])
                    W(t, M['ytrd', 3 if half == 0 else 7])
                    if m >= 2:
                        W(t, M['qevac', half, m - 2])
                    elif half == 1:
                        W(t, M['attnA_oevac'])
                    ps = PS[m % 2].ap()[:, 0:512]
                    mm = None
                    for kc in range(16):
                        mm = t.matmul(ps, qkv_sb8[m % 8][:, kc * P:(kc + 1) * P],
                                      yT[:, kc * TOK + hb: kc * TOK + hb + 512],
                                      start=(kc == 0), stop=(kc == 15))
                    M['qmm', half, m] = tt_.inc(mm)
                TEN(_)
                def _(v, m=m, half=half):
                    W(v, M['qmm', half, m])
                    ps = PS[m % 2].ap()[:, 0:512]
                    store = [QTh, KTh, VTh][m // 16]
                    h = m % 16
                    dst = store.rearrange("p (g hj) -> p g hj", hj=P)[:, :, h * 8:h * 8 + 8]
                    M['qevac', half, m] = vv.inc(v.tensor_copy(dst, ps))
                VEC(_)
                drip(4)

        emit_qkv_half(0)

        psS4 = [ps2.ap()[:, 0:512], ps2.ap()[:, 512:1024], ps0.ap()[:, 0:512]]
        psDx = ps0.ap()[0:1, 512:1024]
        psRx = ps0.ap()[:, 512:1024]
        psV2 = [ps1.ap().bitcast(BF16)[:, 0:128], ps1.ap().bitcast(BF16)[:, 1024:1152]]
        psO2 = [ps3.ap()[:, 0:128], ps3.ap()[:, 512:640]]

        def emit_attn_half(half, ndrip):
            def pk(base, b, j):
                k = (base, half, b - j)
                if k in M:
                    return M[k]
                if half == 1:
                    return M.get((base, 0, 16 - j + b))
                return None

            def pkI(base, b, i, j=1):
                k = (base, half, b, i)
                if k in M:
                    return M[k]
                if half == 1:
                    return M.get((base, 0, 16 + b, i))
                return None

            for it in range(20):
                b = it
                def _(t, b=b, half=half):
                    if b >= 4:
                        W(t, M['recip', half, b - 4])
                        W(t, pkI('rsev', b - 5, 0))
                        M['rbc', half, b - 4] = tt_.inc(t.matmul(
                            psRx, oncF[0:1, :], rbufF, start=True, stop=True))
                    if b < 16:
                        W(t, M['qevac', half, 47])
                        W(t, pk('expdone', b, 3))
                        mm = None
                        for i in range(4):
                            g = 4 * b + i
                            mm = t.matmul(psS4[b % 3][:, i * P:(i + 1) * P],
                                          KTh[:, g * P:(g + 1) * P],
                                          QTh[:, g * P:(g + 1) * P],
                                          start=True, stop=True)
                        M['S', half, b] = tt_.inc(mm)
                    if 1 <= b <= 16:
                        for i in (2, 3):
                            g = 4 * (b - 1) + i
                            W(t, pkI('vevac', b - 1, i - 2))
                            M['trV', half, b - 1, i] = tt_.inc(t.transpose(
                                psV2[i % 2], VTh[:, g * P:(g + 1) * P], idc[:, :]))
                    if b >= 4:
                        b4 = b - 4
                        W(t, M['maskmul', half, b4])
                        for i in (0, 1):
                            W(t, M['vevac', half, b4, i])
                            W(t, pkI('oevac', b4 - 1, i + 2))
                            M['avdone', half, b4, i] = tt_.inc(t.matmul(
                                psO2[i % 2], Vs[i + 4 * (b4 % 4)],
                                Wm[b4 % 4][:, i * P:(i + 1) * P],
                                start=True, stop=True))
                TEN(_)
                if b < 16:
                    def _(s, b=b, half=half):
                        W(s, M['S', half, b])
                        W(s, pk('maskmul', b, 4))
                        W(s, M['eps'])
                        M['expdone', half, b] = ss.inc(s.activation(
                            Wt[b % 4], psS4[b % 3], AF.Exp, bias=nc20, scale=SC))
                    SCA(_)
                def _(v, b=b, half=half):
                    if 1 <= b <= 16:
                        for i in (2, 3):
                            W(v, M['trV', half, b - 1, i])
                            W(v, M.get(('avdone', half, b - 5, i)) or (
                                M.get(('avdone', 0, 11 + b, i)) if half == 1 else None))
                            M['vevac', half, b - 1, i] = vv.inc(v.tensor_copy(
                                Vs[i + 4 * ((b - 1) % 4)], psV2[i % 2]))
                    if b >= 4:
                        b4 = b - 4
                        W(v, M['rbc', half, b4])
                        W(v, pk('oevl', b4, 2))
                        M['rsev', half, b4, 0] = vv.inc(v.tensor_copy(rs[b4 % 2], psRx))
                        v.wait_ge(vv.h, vv.n)
                        for i in (0, 1):
                            g = 4 * b4 + i
                            W(v, M['avdone', half, b4, i])
                            dst = attnT.rearrange("p (h t) -> p h t", h=16)[
                                :, :, half * 512 + g * 8: half * 512 + g * 8 + 8]
                            M['oevac', half, b4, i] = vv.inc(v.tensor_tensor(
                                dst, psO2[i % 2], rs[b4 % 2][:, i * P:(i + 1) * P],
                                ALU.mult))
                VEC(_)
                def _(t, b=b, half=half):
                    if b < 16:
                        for i in (0, 1):
                            g = 4 * b + i
                            W(t, pkI('vevac', b - 1, i + 2))
                            M['trV', half, b, i] = tt_.inc(t.transpose(
                                psV2[i % 2], VTh[:, g * P:(g + 1) * P], idc[:, :]))
                    if 3 <= b < 19:
                        b3 = b - 3
                        W(t, M['maskmul', half, b3])
                        W(t, pkI('rsev', b3 - 1, 0))
                        M['denom', half, b3] = tt_.inc(t.matmul(
                            psDx, oncol[:, :], Wm[b3 % 4], start=True, stop=True))
                    if b >= 4:
                        b4 = b - 4
                        for i in (2, 3):
                            W(t, M['vevac', half, b4, i])
                            W(t, pkI('oevac', b4, i - 2))
                            M['avdone', half, b4, i] = tt_.inc(t.matmul(
                                psO2[i % 2], Vs[i + 4 * (b4 % 4)],
                                Wm[b4 % 4][:, i * P:(i + 1) * P],
                                start=True, stop=True))
                        M['avlast', half, b4] = M['avdone', half, b4, 3]
                TEN(_)
                if 3 <= b < 19:
                    def _(s, b=b, half=half):
                        b3 = b - 3
                        if b == 3 and half == 0:
                            W(s, M['ytrd', 7])
                        W(s, M['denom', half, b3])
                        W(s, pk('rbc', b3, 1))
                        ss.inc(s.activation(rb2F, psDx, AF.Ln))
                        s.wait_ge(ss.h, ss.n)
                        M['recip', half, b3] = ss.inc(s.activation(
                            rbufF, rb2F, AF.Exp, scale=-1.0))
                    SCA(_)
                def _(v, b=b, half=half):
                    if b < 16:
                        for i in (0, 1):
                            W(v, M['trV', half, b, i])
                            W(v, M.get(('avdone', half, b - 4, i)) or (
                                M.get(('avdone', 0, 12 + b, i)) if half == 1 else None))
                            M['vevac', half, b, i] = vv.inc(v.tensor_copy(
                                Vs[i + 4 * (b % 4)], psV2[i % 2]))
                        W(v, M['expdone', half, b])
                        W(v, pk('avlast', b, 4))
                        mrep = bass.AP(mkc.ap().tensor, 0, [[P, P], [0, 4], [1, P]])
                        M['maskmul', half, b] = vv.inc(
                            v.tensor_tensor(Wm[b % 4], Wt[b % 4], mrep, ALU.mult))
                    if b >= 4:
                        b4 = b - 4
                        W(v, M['rsev', half, b4, 0])
                        for i in (2, 3):
                            g = 4 * b4 + i
                            W(v, M['avdone', half, b4, i])
                            dst = attnT.rearrange("p (h t) -> p h t", h=16)[
                                :, :, half * 512 + g * 8: half * 512 + g * 8 + 8]
                            M['oevac', half, b4, i] = vv.inc(v.tensor_tensor(
                                dst, psO2[i % 2], rs[b4 % 2][:, i * P:(i + 1) * P],
                                ALU.mult))
                        M['oevl', half, b4] = M['oevac', half, b4, 3]
                VEC(_)
                if b < 16:
                    drip(ndrip)

        emit_attn_half(0, 2)
        M['attnA_end'] = tt_.now()
        M['attnA_oevac'] = vv.now()

        emit_qkv_half(1)

        emit_attn_half(1, 2)
        M['attnB_end'] = tt_.now()
        M['attnB_oevac'] = vv.now()

        def _(sc_):
            for kc in range(4):
                W(sc_, M['wobf', 15])
                W(sc_, M['qmm', 1, 44 + kc])
                M['wopre', kc] = s_qw[kc].incd(sc_.dma_start(
                    out=qkv_sb[kc][:, :], in_=wout_bf.ap()[kc * P:(kc + 1) * P, :]))
        SCA(_)

        def emit_cc(name, ins_t, outs_t, sem, wms):
            def _(g):
                W(g, wms)
                M['cc', name] = sem.inc(g.collective_compute(
                    "AllGather", ALU.bypass, replica_groups=[list(range(NCORES))],
                    ins=[ins_t.ap().opt()], outs=[outs_t.ap().opt()]))
            GPS(_)

        drip_finish('gate')
        emit_cc('gate', gTsh, wgT, s_ccg, M['rawdone', 'gate'])
        drip_finish('up')
        emit_cc('up', uTsh, wuT, s_ccu, M['rawdone', 'up'])

        groups8 = [ps0.ap()[:, 0:512], ps0.ap()[:, 512:1024],
                   ps1.ap()[:, 0:512], ps1.ap()[:, 512:1024],
                   ps2.ap()[:, 0:512], ps2.ap()[:, 512:1024],
                   ps3.ap()[:, 0:512], ps3.ap()[:, 512:1024]]
        LN2 = {tt: {k: slot(96 + 8 * tt + j) for j, k in enumerate(
            ['negm', 'lnv', 'rstd', 'negmr'])} for tt in range(NT)}
        LN3 = {tt: {k: slot(320 + 8 * tt + j) for j, k in enumerate(
            ['negm', 'lnv', 'rstd', 'negmr'])} for tt in range(NT)}
        LN4 = {tt: {k: slot(440 + 4 * tt + j) for j, k in enumerate(
            ['negm', 'lnv', 'rstd', 'negmr'])} for tt in range(NT)}

        for pair in range(4):
            tts = (2 * pair, 2 * pair + 1)
            order = list(range(16)) if pair % 2 == 0 else list(range(15, -1, -1))
            def _(sy, pair=pair, tts=tts):
                for tt in tts:
                    if tt >= 2:
                        W(sy, M['zbf', tt - 2])
                        W(sy, M['spill', tt - 2])
                    else:
                        W(sy, M['qmm', 1, 47])
                    M['x5', tt] = s_x[tt % 2].incd(sy.dma_start(
                        out=xrot[tt % 2][:, :], in_=x_in[tt * P:(tt + 1) * P, :]))
            SY(_)
            for oi, kc in enumerate(order):
                pre0 = (pair == 0 and kc < 4)
                needs_load = (kc >= 4) if pair == 0 else (oi >= 8)
                if needs_load:
                    def _(eng, pair=pair, kc=kc):
                        W(eng, M['wobf', 15])
                        if pair == 0 and kc < 12:
                            W(eng, M['attnB_end'])
                        k = ('opk', pair, kc ^ 8)
                        if k in M:
                            W(eng, M[k])
                        M['wo', pair, kc] = s_wo8[kc % 8].incd(
                            eng.dma_start(out=woutr[kc % 8][:, :],
                                          in_=wout_bf.ap()[kc * P:(kc + 1) * P, :]))
                    if kc % 2 == 0:
                        SCA(_)
                    else:
                        GPS(_)
                def _(t, pair=pair, kc=kc, oi=oi, tts=tts, pre0=pre0,
                      needs_load=needs_load):
                    if pre0:
                        W(t, M['wopre', kc])
                    elif needs_load:
                        W(t, M['wo', pair, kc])
                    if oi == 0:
                        W(t, M['attnB_end'])
                        W(t, M['attnB_oevac'])
                        if pair > 0:
                            W(t, M['s1evac', 2 * pair - 1])
                    src = qkv_sb[kc] if pre0 else woutr[kc % 8]
                    mm = None
                    for ti, tt in enumerate(tts):
                        for n4 in range(4):
                            mm = t.matmul(groups8[ti * 4 + n4],
                                          attnT[:, kc * TOK + tt * P: kc * TOK + (tt + 1) * P],
                                          src[:, n4 * 512:(n4 + 1) * 512],
                                          start=(oi == 0), stop=False)
                    M['opk', pair, kc] = tt_.inc(mm)
                TEN(_)

            def _(t, pair=pair, tts=tts, order=order):
                W(t, M['boutbf'])
                W(t, M['opk', pair, order[-1]])
                for ti, tt in enumerate(tts):
                    for n4 in range(4):
                        M['opd', tt, n4] = tt_.inc(t.matmul(
                            groups8[ti * 4 + n4], onc[0:1, :],
                            bout_b[0:1, n4 * 512:(n4 + 1) * 512], start=False, stop=True))
            TEN(_)
            for ti, tt in enumerate(tts):
                def _(v, tt=tt, ti=ti):
                    c = None
                    for n4 in range(4):
                        W(v, M['opd', tt, n4])
                        c = v.tensor_copy(s1b[ti][:, n4 * 512:(n4 + 1) * 512], groups8[ti * 4 + n4])
                    M['s1evac', tt] = vv.inc(c)
                VEC(_)
            def _(v, tts=tts):
                for ti, tt in enumerate(tts):
                    L2 = LN2[tt]
                    W(v, M['s1evac', tt])
                    b = None
                    for c4 in range(4):
                        b = v.bn_stats(bnst[:, c4 * 6:(c4 + 1) * 6],
                                       s1b[ti][:, c4 * 512:(c4 + 1) * 512])
                    vv.inc(b)
                    v.wait_ge(vv.h, vv.n)
                    vv.inc(v.bn_aggr(mv2t[tt], bnst))
                    v.wait_ge(vv.h, vv.n)
                    M['mv2', tt] = vv.inc(
                        v.tensor_scalar(L2['negm'], mv2t[tt][:, 0:1], -1.0, None, ALU.mult))
            VEC(_)
            def _(s, tts=tts):
                for tt in tts:
                    L2 = LN2[tt]
                    W(s, M['mv2', tt])
                    M['rstd2', tt] = ss.inc(s.activation(
                        L2['lnv'], mv2t[tt][:, 1:2], AF.Sqrt, bias=epsc, scale=1.0))
            SCA(_)
            def _(v, tts=tts):
                for ti, tt in enumerate(tts):
                    L2 = LN2[tt]
                    W(v, M['rstd2', tt])
                    vv.inc(v.reciprocal(L2['rstd'], L2['lnv']))
                    v.wait_ge(vv.h, vv.n)
                    vv.inc(v.tensor_scalar(L2['negmr'], L2['negm'], L2['rstd'], None, ALU.mult))
                    if tt >= 2:
                        W(v, M['r1', tt - 2])
                    v.wait_ge(vv.h, vv.n)
                    M['n2', tt] = vv.inc(v.tensor_scalar(
                        [r1rA, r1rB][tt % 2], s1b[ti], L2['rstd'], L2['negmr'],
                        ALU.mult, ALU.add))
                for tt in tts:
                    W(v, M['n2', tt])
                    W(v, M['x5', tt])
                    M['r1', tt] = vv.inc(v.tensor_tensor(
                        xrot[tt % 2][:, :], xrot[tt % 2][:, :],
                        [r1rA, r1rB][tt % 2], ALU.add))
            VEC(_)
            def _(v, tts=tts):
                for tt in tts:
                    L3 = LN3[tt]
                    W(v, M['r1', tt])
                    b = None
                    for c4 in range(4):
                        b = v.bn_stats(bnst[:, c4 * 6:(c4 + 1) * 6],
                                       xrot[tt % 2][:, c4 * 512:(c4 + 1) * 512])
                    vv.inc(b)
                    v.wait_ge(vv.h, vv.n)
                    vv.inc(v.bn_aggr(mv3t[tt], bnst))
                    v.wait_ge(vv.h, vv.n)
                    M['mv3', tt] = vv.inc(
                        v.tensor_scalar(L3['negm'], mv3t[tt][:, 0:1], -1.0, None, ALU.mult))
            VEC(_)
            def _(s, tts=tts):
                for tt in tts:
                    L3 = LN3[tt]
                    W(s, M['mv3', tt])
                    M['rstd3', tt] = ss.inc(s.activation(
                        L3['lnv'], mv3t[tt][:, 1:2], AF.Sqrt, bias=epsc, scale=1.0))
            SCA(_)
            def _(v, tts=tts):
                for tt in tts:
                    L3 = LN3[tt]
                    W(v, M['rstd3', tt])
                    vv.inc(v.reciprocal(L3['rstd'], L3['lnv']))
                    v.wait_ge(vv.h, vv.n)
                    vv.inc(v.tensor_scalar(L3['negmr'], L3['negm'], L3['rstd'], None, ALU.mult))
                    v.wait_ge(vv.h, vv.n)
                    M['zbf', tt] = vv.inc(v.tensor_scalar(
                        ztile[tt], xrot[tt % 2][:, :], L3['rstd'], L3['negmr'],
                        ALU.mult, ALU.add))
            VEC(_)
            def _(sy, tts=tts):
                for tt in tts:
                    W(sy, M['zbf', tt])
                    M['spill', tt] = [s_sp, s_sp2][tt % 2].incd(sy.dma_start(
                        out=r1d.ap()[tt * P:(tt + 1) * P, :], in_=xrot[tt % 2][:, :]))
            SY(_)
            drip(20)

        def _(t):
            W(t, M['s1evac', 7])
        TEN(_)
        for tt in range(NT):
            emit_tr16(M['zbf', tt], ztile[tt], zT, tt, 'ztr16')
        M['ztr_all'] = tr_evac[-1]

        if debug:
            def _(sy):
                W(sy, M['ytrd', 7])
                s_cst.incd(sy.dma_start(out=taps['tap_yT'], in_=yT))
                W(sy, M['attnB_oevac'])
                s_cst.incd(sy.dma_start(out=taps['tap_attnT'], in_=attnT))
                for tt in range(NT):
                    W(sy, M['spill', tt])
                s_cst.incd(sy.dma_start(out=taps['tap_r1'], in_=r1d.ap()))
                W(sy, M['ztr_all'])
                s_cst.incd(sy.dma_start(out=taps['tap_zT'], in_=zT))
                W(sy, M['cc', 'gate'])
                s_cst.incd(sy.dma_start(out=taps['tap_wg'], in_=wgT.ap()))
                W(sy, M['cc', 'down'])
                M['dbg1'] = s_cst.incd(sy.dma_start(out=taps['tap_wd'], in_=wdF.ap()))
            SY(_)

        for ot in range(NOT):
            gslot = [ps0, ps2][ot % 2]
            uslot = [ps1, ps3][ot % 2]
            def _(sy, ot=ot):
                if ot == 0:
                    W(sy, M['cc', 'gate'])
                    W(sy, M['cc', 'up'])
                    W(sy, M['n2', 7])
                    W(sy, M['opd', 7, 3])
                if ot >= 2:
                    W(sy, M['gumm', ot - 2])
                s_gu[ot % 2].incd(sy.dma_start(
                    out=gate_sb[ot % 2].rearrange("p (a c) -> p a c", c=P),
                    in_=wgT.ap().rearrange("(a p) o -> p a o", p=P)[:, :, ot * P:(ot + 1) * P]))
                M['guld', ot] = s_gu[ot % 2].incd(sy.dma_start(
                    out=up_sb[ot % 2].rearrange("p (a c) -> p a c", c=P),
                    in_=wuT.ap().rearrange("(a p) o -> p a o", p=P)[:, :, ot * P:(ot + 1) * P]))
            SY(_)
            def _(t, ot=ot, gslot=gslot, uslot=uslot):
                W(t, M['guld', ot])
                if ot == 0:
                    W(t, M['ztr_all'])
                if ot >= 2:
                    W(t, M['hact', ot - 2])
                mm = None
                for kc in range(16):
                    t.matmul(gslot.ap()[:, 0:512], gate_sb[ot % 2][:, kc * P:(kc + 1) * P],
                             zT[:, kc * TOK: kc * TOK + 512], start=(kc == 0), stop=(kc == 15))
                    t.matmul(gslot.ap()[:, 512:1024], gate_sb[ot % 2][:, kc * P:(kc + 1) * P],
                             zT[:, kc * TOK + 512: kc * TOK + 1024], start=(kc == 0),
                             stop=(kc == 15))
                for kc in range(16):
                    t.matmul(uslot.ap()[:, 0:512], up_sb[ot % 2][:, kc * P:(kc + 1) * P],
                             zT[:, kc * TOK: kc * TOK + 512], start=(kc == 0), stop=(kc == 15))
                    mm = t.matmul(uslot.ap()[:, 512:1024], up_sb[ot % 2][:, kc * P:(kc + 1) * P],
                                  zT[:, kc * TOK + 512: kc * TOK + 1024], start=(kc == 0),
                                  stop=(kc == 15))
                M['gumm', ot] = tt_.inc(mm)
            TEN(_)
            def _(s, ot=ot, gslot=gslot, uslot=uslot):
                W(s, M['gumm', ot])
                if ot >= 2:
                    W(s, M['hact', ot - 2])
                M['silu', ot] = ss.inc(s.activation(sgb2[ot % 2], gslot.ap()[:, :], AF.Silu))
            SCA(_)
            def _(v, ot=ot, uslot=uslot):
                W(v, M['silu', ot])
                if ot >= 2:
                    W(v, M['hact', ot - 2])
                vv.inc(v.tensor_copy(ubuf2[ot % 2], uslot.ap()[:, :]))
                v.wait_ge(vv.h, vv.n)
                M['hact', ot] = vv.inc(v.tensor_tensor(
                    h_act[:, ot * TOK:(ot + 1) * TOK], sgb2[ot % 2], ubuf2[ot % 2], ALU.mult))
            VEC(_)
            drip_down(6)

        while gdcursor[0] < len(gitems_d):
            drip_down(8)

        if debug:
            def _(sy):
                W(sy, M['hact', NOT - 1])
                M['dbg2'] = s_cst.incd(sy.dma_start(out=taps['tap_h'], in_=h_act))
            SY(_)

        dw_ld = []
        dw_mm = []

        def emit_p8_group(tg):
            for nqp in range(2):
                for kc in range(NOT):
                    def _(sy, tg=tg, nqp=nqp, kc=kc):
                        if len(dw_ld) < 2:
                            W(sy, M['cc', 'down'])
                            W(sy, M['gumm', NOT - 1])
                        if len(dw_mm) >= 4:
                            W(sy, dw_mm[-4])
                        sl = len(dw_ld) % 4
                        M['dwl', tg, nqp, kc] = s_dw[sl].incd(sy.dma_start(
                            out=wd_sb[sl][:, :],
                            in_=wdF.ap()[nqp * 1024:(nqp + 1) * 1024,
                                         kc * P:(kc + 1) * P],
                            transpose=True))
                        dw_ld.append(M['dwl', tg, nqp, kc])
                    SY(_)
                    def _(t, tg=tg, nqp=nqp, kc=kc):
                        sl = len(dw_mm) % 4
                        W(t, M['dwl', tg, nqp, kc])
                        if kc == 0:
                            if tg == 0 and nqp == 0:
                                W(t, M['hact', NOT - 1])
                                W(t, M['silu', NOT - 1])
                            elif nqp == 0:
                                W(t, M['devac', 0, 1, 3])
                            else:
                                W(t, M['devac', tg, 0, tg * 4 + 3])
                        mm = None
                        for j in range(4):
                            tt = tg * 4 + j
                            for q in range(2):
                                mm = t.matmul(groups8[j * 2 + q],
                                              h_act[:, kc * TOK + tt * P: kc * TOK + (tt + 1) * P],
                                              wd_sb[sl][:, q * 512:(q + 1) * 512],
                                              start=(kc == 0), stop=(kc == NOT - 1))
                        M['dwk', tg, nqp, kc] = tt_.inc(mm)
                        dw_mm.append(M['dwk', tg, nqp, kc])
                    TEN(_)
                def _(v, tg=tg, nqp=nqp):
                    W(v, M['dwk', tg, nqp, NOT - 1])
                    if tg == 0 and nqp == 0:
                        W(v, M['rawdone', 'down'])
                    if tg == 1 and nqp == 0:
                        W(v, M['spill', 6])
                        W(v, M['spill', 7])
                    for j in range(4):
                        tt = tg * 4 + j
                        for q in range(2):
                            nq = nqp * 2 + q
                            M['devac', tg, nqp, tt] = vv.inc(v.tensor_copy(
                                d_bf[:, tt * 2048 + nq * 512: tt * 2048 + (nq + 1) * 512],
                                groups8[j * 2 + q]))
                            v.wait_ge(vv.h, vv.n)
                            M['dst', nq, tt] = vv.inc(v.bn_stats(
                                stt[:, 256 + tt * 24 + nq * 6: 256 + tt * 24 + (nq + 1) * 6],
                                d_bf[:, tt * 2048 + nq * 512: tt * 2048 + (nq + 1) * 512]))
                VEC(_)

        def emit_p9_tile(tt):
            L4 = LN4[tt]
            def _(sc_, tt=tt):
                W(sc_, M['spill', tt])
                if tt < 2:
                    W(sc_, M['r1', 7])
                    W(sc_, M['zbf', 7])
                    W(sc_, M['spill', 7])
                else:
                    W(sc_, M['outadd', tt - 2])
                M['r1re', tt] = s_gu[tt % 2].incd(sc_.dma_start(
                    out=r1ld[tt % 2][:, :], in_=r1d.ap()[tt * P:(tt + 1) * P, :]))
            SCA(_)
            def _(v, tt=tt, L4=L4):
                W(v, M['dst', 3, tt])
                vv.inc(v.bn_aggr(mv4t[tt], stt[:, 256 + tt * 24: 256 + (tt + 1) * 24]))
                v.wait_ge(vv.h, vv.n)
                M['mv4', tt] = vv.inc(
                    v.tensor_scalar(L4['negm'], mv4t[tt][:, 0:1], -1.0, None, ALU.mult))
            VEC(_)
            def _(s, tt=tt, L4=L4):
                W(s, M['mv4', tt])
                M['rstd4', tt] = ss.inc(s.activation(
                    L4['lnv'], mv4t[tt][:, 1:2], AF.Sqrt, bias=epsc, scale=1.0))
            SCA(_)
            def _(v, tt=tt, L4=L4):
                W(v, M['rstd4', tt])
                vv.inc(v.reciprocal(L4['rstd'], L4['lnv']))
                v.wait_ge(vv.h, vv.n)
                vv.inc(v.tensor_scalar(L4['negmr'], L4['negm'], L4['rstd'], None, ALU.mult))
                if tt >= 1:
                    W(v, M['outadd', tt - 1])
                v.wait_ge(vv.h, vv.n)
                M['n4', tt] = vv.inc(v.tensor_scalar(
                    n4buf[:, 0:2048], d_bf[:, tt * 2048:(tt + 1) * 2048],
                    L4['rstd'], L4['negmr'], ALU.mult, ALU.add))
            VEC(_)
            def _(v, tt=tt):
                W(v, M['n4', tt])
                W(v, M['r1re', tt])
                if tt >= 2:
                    W(v, M['out', tt - 2])
                M['outadd', tt] = vv.inc(v.tensor_tensor(
                    outbufs[tt % 2][:, 0:2048], r1ld[tt % 2][:, :], n4buf[:, 0:2048], ALU.add))
            VEC(_)
            def _(sc_, tt=tt):
                W(sc_, M['outadd', tt])
                M['out', tt] = s_out[tt % 2].incd(sc_.dma_start(
                    out=y_out[tt * P:(tt + 1) * P, :], in_=outbufs[tt % 2][:, 0:2048]))
            SCA(_)

        emit_p8_group(0)
        for tt in range(4):
            emit_p9_tile(tt)
        emit_p8_group(1)
        for tt in range(4, NT):
            emit_p9_tile(tt)

        def _(sy):
            W(sy, M['out', NT - 2])
            W(sy, M['out', NT - 1])
            if debug:
                W(sy, M.get('dbg1'))
                W(sy, M.get('dbg2'))
        SY(_)

    return nc


_NC_CACHE = {}


def _get_nc(debug=False):
    if debug not in _NC_CACHE:
        _NC_CACHE[debug] = build_nc(debug)
    return _NC_CACHE[debug]


def make_in_maps(L0_attn_x, L0_attn_W_qkv, L0_attn_W_out, L0_attn_b_out,
                 L0_ffn_lut, L0_ffn_gate_idx, L0_ffn_up_idx, L0_ffn_down_idx):
    x = np.ascontiguousarray(np.asarray(L0_attn_x, dtype=np.float32).reshape(8192, D))
    wq0 = np.asarray(L0_attn_W_qkv, dtype=np.float32)
    wq = np.ascontiguousarray(
        wq0.reshape(16, 128, 48, 128).transpose(2, 1, 0, 3).reshape(6144, 2048))
    wo = np.ascontiguousarray(np.asarray(L0_attn_W_out, dtype=np.float32))
    bo = np.ascontiguousarray(np.asarray(L0_attn_b_out, dtype=np.float32))
    lu = np.ascontiguousarray(np.asarray(L0_ffn_lut, dtype=np.float32))
    gi = np.asarray(L0_ffn_gate_idx, dtype=np.int32)
    ui = np.asarray(L0_ffn_up_idx, dtype=np.int32)
    di = np.asarray(L0_ffn_down_idx, dtype=np.int32)
    gi = np.ascontiguousarray(gi).view(np.uint8).reshape(gi.shape[0], gi.shape[1], 4)[:, :, 0].astype(np.int8)
    ui = np.ascontiguousarray(ui).view(np.uint8).reshape(ui.shape[0], ui.shape[1], 4)[:, :, 0].astype(np.int8)
    di = np.ascontiguousarray(di).view(np.uint8).reshape(di.shape[0], di.shape[1], 4)[:, :, 0].astype(np.int8)
    ident = np.eye(P, dtype=np.float32).astype(bfnp)
    mask = (np.arange(P)[:, None] % 8 == np.arange(P)[None, :] % 8).astype(bfnp)
    ones = np.ones((1, P), dtype=np.float32).astype(bfnp)
    maps = []
    for c in range(NCORES):
        maps.append(dict(
            x_in=np.ascontiguousarray(x[c * TOK:(c + 1) * TOK]),
            wqkv_my=np.ascontiguousarray(wq[c * 768:(c + 1) * 768]),
            wout=wo, bout=bo, lut=lu,
            gidx=np.ascontiguousarray(gi[:, c * SHC:(c + 1) * SHC].T),
            uidx=np.ascontiguousarray(ui[:, c * SHC:(c + 1) * SHC].T),
            didx=np.ascontiguousarray(di[c * SHC:(c + 1) * SHC, :]),
            identc=ident, maskc=mask, onesc=ones,
        ))
    return maps


def kernel(L0_attn_x, L0_attn_W_qkv, L0_attn_W_out, L0_attn_b_out,
           L0_ffn_lut, L0_ffn_gate_idx, L0_ffn_up_idx, L0_ffn_down_idx):
    nc = _get_nc(False)
    maps = make_in_maps(L0_attn_x, L0_attn_W_qkv, L0_attn_W_out, L0_attn_b_out,
                        L0_ffn_lut, L0_ffn_gate_idx, L0_ffn_up_idx, L0_ffn_down_idx)
    res = run_bass_kernel_spmd(nc, maps, core_ids=list(range(NCORES)))
    out = np.concatenate([np.asarray(r["y_out"]) for r in res.results], axis=0)
    return out.reshape(4, 2048, 2048).astype(np.float32)
